# version 06
# baseline (speedup 1.0000x reference)
import numpy as np
from contextlib import ExitStack

import concourse.bass as bass
import concourse.mybir as mybir
from concourse.bass_utils import run_bass_kernel_spmd

F32 = mybir.dt.float32
F32R = mybir.dt.float32r
BF16 = mybir.dt.bfloat16
AF = mybir.ActivationFunctionType
ALU = mybir.AluOpType
AX = mybir.AxisListType

P = 128
D = 2048
TT = 16
DT = 16
H = 16
E = 128
NCH = 16
MS = 12288
NMS = 48
MSW = 256
HID = 8192
NS = 64
NQ = 4
QW = 512
TSUP = 4
LNEPS = 1e-5
RMSEPS = 1e-6


class Sem:
    def __init__(self, nc, es, name):
        self.h = es.enter_context(nc.semaphore(name))
        self.n = 0
        self.name = name


class Prog:
    ENGS = ("tensor", "vector", "scalar", "sync", "gpsimd")

    def __init__(self, nc):
        self.nc = nc
        self.q = {e: [] for e in self.ENGS}
        self.last_wait = {}

    def _wait(self, eng, sem, val):
        if val <= 0:
            return
        key = (eng, sem.name)
        if self.last_wait.get(key, 0) >= val:
            return
        self.last_wait[key] = val
        self.q[eng].append(lambda e, h=sem.h, v=val: e.wait_ge(h, v))

    def op(self, eng, fn, waits=(), inc=None, inc_k=1):
        for sem, val in waits:
            self._wait(eng, sem, val)
        if inc is not None:
            self.q[eng].append(lambda e, fn=fn, h=inc.h, k=inc_k: fn(e).then_inc(h, k))
            inc.n += inc_k
            return inc.n
        self.q[eng].append(fn)
        return None

    def barrier(self, sems):
        for eng in self.ENGS:
            for s in sems:
                self._wait(eng, s, s.n)

    def flush(self):
        with self.nc.Block() as block:
            for eng in self.ENGS:
                ops = self.q[eng]
                if ops:
                    getattr(block, eng)(lambda e, ops=ops: [f(e) for f in ops])
        self.q = {e: [] for e in self.ENGS}


def build(debug=False):
    nc = bass.Bass("TRN2", target_bir_lowering=False)
    import ml_dtypes

    x_d = nc.dram_tensor("x", [TT, P, D], F32, kind="ExternalInput").ap()
    wmod_d = nc.dram_tensor("wmod", [NMS, DT, P, MSW], F32, kind="ExternalInput").ap()
    ccol_d = nc.dram_tensor("ccol", [P, DT], F32, kind="ExternalInput").ap()
    wqkv_d = nc.dram_tensor("wqkv", [H, 3, DT, P, E], F32, kind="ExternalInput").ap()
    wf_d = nc.dram_tensor("wf", [DT, P, H], F32, kind="ExternalInput").ap()
    wo_d = nc.dram_tensor("wo", [DT, P, D], F32, kind="ExternalInput").ap()
    nw_d = nc.dram_tensor("nwcol", [P, DT], F32, kind="ExternalInput").ap()
    wg_d = nc.dram_tensor("wg", [NS, 2, DT, P, E], F32, kind="ExternalInput").ap()
    wl_d = nc.dram_tensor("wl", [NQ, NS, P, QW], F32, kind="ExternalInput").ap()
    y_d = nc.dram_tensor("y", [TT, P, D], F32, kind="ExternalOutput").ap()

    x1_hbm = nc.dram_tensor("x1_scratch", [TT, P, D], F32, kind="Internal").ap()
    act_hbm = nc.dram_tensor("act_scratch", [TSUP, NS, P, QW], BF16, kind="Internal").ap()

    eye_bf_d = nc.inline_tensor(np.eye(P, dtype=ml_dtypes.bfloat16), "eye_bf_c")
    eye16_d = nc.inline_tensor(np.eye(16, dtype=np.float32), "eye16_c")
    tri_np = np.triu(np.ones((P, P), dtype=np.float32))
    tri_d = nc.inline_tensor(tri_np, "tri_c")
    mask_d = nc.inline_tensor(tri_np.astype(ml_dtypes.bfloat16), "mask_c")
    ones_f_d = nc.inline_tensor(np.ones((1, P), dtype=np.float32), "ones_f_c")
    ones_b_d = nc.inline_tensor(np.ones((1, P), dtype=ml_dtypes.bfloat16), "ones_b_c")
    onescol_d = nc.inline_tensor(np.ones((P, 1), dtype=np.float32), "ones_col_c")
    constvals = [LNEPS, RMSEPS]
    constv_d = nc.inline_tensor(
        np.broadcast_to(np.asarray(constvals, np.float32), (P, len(constvals))).copy(),
        "constv_c")

    dbg = {}
    es = ExitStack()

    def sbl(name, shape, dt):
        return es.enter_context(nc.sbuf_tensor(name, shape, dt, side="left"))

    eye_bf = sbl("eye_bf", [P, P], BF16)
    eye16 = sbl("eye16", [16, 16], F32)
    tri = sbl("tri", [P, P], F32)
    maskb = sbl("maskb", [P, P], BF16)
    ones_f = sbl("ones_f", [1, P], F32)
    ones_b = sbl("ones_b", [1, P], BF16)
    sacol = sbl("sacol", [P, DT], F32)
    cacol = sbl("cacol", [P, DT], F32)
    sbcol = sbl("sbcol", [P, DT], F32)
    cbcol = sbl("cbcol", [P, DT], F32)
    garow = sbl("garow", [1, D], BF16)
    gbrow = sbl("gbrow", [1, D], BF16)
    mean1 = sbl("mean1", [P, TT], F32)
    rstd1 = sbl("rstd1", [P, TT], F32)
    mean2 = sbl("mean2", [P, TT], F32)
    rstd2 = sbl("rstd2", [P, TT], F32)
    msum = sbl("msum", [P, TT], F32)
    sqsum = sbl("sqsum", [P, TT], F32)
    ssq = sbl("ssq", [P, TT], F32)
    ssqt = sbl("ssqt", [P, 1], F32)
    rmsr = sbl("rmsr", [P, TT], F32)
    cumall = sbl("cumall", [P, NCH * H], F32)
    e1t = sbl("e1t", [P, NCH * H], F32)
    e2t = sbl("e2t", [P, NCH * H], F32)
    e3t = sbl("e3t", [P, (NCH - 1) * H], F32)
    e4bc = sbl("e4bc", [P, (NCH - 1) * H], F32)
    ldtok = sbl("ldtok", [P, NCH * H], F32)
    Sbf = sbl("Sbf", [P, E], BF16)
    tmp1 = sbl("tmp1", [P, NCH * H], F32)
    nwcol = sbl("nwcol_sb", [P, DT], F32)
    oT = sbl("oT", [P, H * D], BF16)
    constv = sbl("constv", [P, 2], F32)
    onescol = sbl("onescol", [P, 1], F32)
    totrow = sbl("totrow", [1, NCH * H], F32)
    nc.const_aps.register(F32, LNEPS, constv[:, 0:1])
    nc.const_aps.register(F32, RMSEPS, constv[:, 1:2])

    sT = Sem(nc, es, "sT")
    sV = Sem(nc, es, "sV")
    sS = Sem(nc, es, "sS")
    dA = Sem(nc, es, "dA")
    dB = Sem(nc, es, "dB")
    dC = Sem(nc, es, "dC")
    dE = Sem(nc, es, "dE")
    dF = Sem(nc, es, "dF")
    dG = Sem(nc, es, "dG")
    dOut = Sem(nc, es, "dOut")
    dOut2 = Sem(nc, es, "dOut2")
    dH = Sem(nc, es, "dH")
    dI = Sem(nc, es, "dI")
    ALLSEMS = [sT, sV, sS, dA, dB, dC, dE, dF, dG, dOut, dOut2, dH, dI]

    pr = Prog(nc)

    def dma(qeng, dst, src, sem, waits=()):
        return pr.op(qeng, lambda e, d=dst, s=src: e.dma_start(out=d, in_=s),
                     waits=waits, inc=sem, inc_k=16)

    def BARRIER():
        pr.barrier(ALLSEMS)

    _sc1 = ExitStack()
    wmA = _sc1.enter_context(nc.sbuf_tensor("wmbufA", [P, DT * MSW], BF16, side="right"))
    wmB = _sc1.enter_context(nc.sbuf_tensor("wmbufB", [P, DT * MSW], BF16, side="right"))
    modrow = _sc1.enter_context(nc.sbuf_tensor("modrow", [1, MS], F32, side="right"))
    ccol = _sc1.enter_context(nc.sbuf_tensor("ccol_sb", [P, DT], F32, side="right"))
    silc = _sc1.enter_context(nc.sbuf_tensor("silc", [P, DT], F32, side="right"))
    silcb = _sc1.enter_context(nc.sbuf_tensor("silcb", [P, DT], BF16, side="right"))
    grow = _sc1.enter_context(nc.sbuf_tensor("grow", [1, D], F32, side="right"))
    grow2 = _sc1.enter_context(nc.sbuf_tensor("grow2", [1, D], F32, side="right"))
    xrA = _sc1.enter_context(nc.sbuf_tensor("xrA", [P, D], F32, side="right"))
    xrB = _sc1.enter_context(nc.sbuf_tensor("xrB", [P, D], F32, side="right"))
    junk1 = _sc1.enter_context(nc.sbuf_tensor("junk1", [P, D], BF16, side="right"))
    ps_m0 = _sc1.enter_context(nc.psum_tensor("ps_m0", [P, 512], F32))
    ps_m1 = _sc1.enter_context(nc.psum_tensor("ps_m1", [P, 512], F32))
    ps_col = _sc1.enter_context(nc.psum_tensor("ps_col", [P, 512], F32))
    if True:

        for dst, src in ((eye_bf, eye_bf_d), (eye16, eye16_d), (tri, tri_d),
                         (maskb, mask_d), (ones_f, ones_f_d), (ones_b, ones_b_d)):
            dma("sync", dst[:, :], src.ap(), dC)
        dma("sync", constv[:, :], constv_d.ap(), dC)
        dma("sync", onescol[:, :], onescol_d.ap(), dC)
        dma("sync", ccol[:, :], ccol_d, dC)
        dma("sync", nwcol[:, :], nw_d, dC)
        cdone = dC.n

        sil_done = pr.op("scalar", lambda e: e.activation(
            out=silc[:, :], in_=ccol[:, :], func=AF.Silu),
            waits=[(dC, cdone)], inc=sS)
        silb_done = pr.op("vector", lambda e: e.tensor_copy(
            out=silcb[:, :], in_=silc[:, :]), waits=[(sS, sil_done)], inc=sV)

        xrP = (xrA, xrB)
        xsemP = (dC, dE)
        vdoneP = [0, 0]
        sdoneP = [0, 0]
        for t in range(TT):
            par = t % 2
            w = ([(sV, vdoneP[par]), (sS, sdoneP[par])] if t >= 2
                 else [(dC, cdone)])
            cnt = dma("sync", xrP[par][:, :], x_d[t], xsemP[par], waits=w)
            vdoneP[par] = pr.op("vector", lambda e, par=par, t=t: e.tensor_reduce(
                out=msum[:, t:t + 1], in_=xrP[par][:, :], axis=AX.X, op=ALU.add),
                waits=[(xsemP[par], cnt)], inc=sV)
            sdoneP[par] = pr.op("scalar", lambda e, par=par, t=t: e.activation(
                out=junk1[:, :], in_=xrP[par][:, :], func=AF.Square,
                accum_out=sqsum[:, t:t + 1]),
                waits=[(xsemP[par], cnt), (sS, max(sdoneP))], inc=sS)

        mv_done = []
        ev_par = [0, 0]
        for ms in range(NMS):
            par = ms % 2
            buf = wmA if par == 0 else wmB
            dsem = dF if par == 0 else dG
            w = [(sT, mv_done[-2])] if len(mv_done) >= 2 else []
            cnt = dma("gpsimd", buf[:, :].rearrange("p (dt c) -> p dt c", dt=DT),
                      wmod_d[ms].rearrange("dt p c -> p dt c"), dsem, waits=w)
            psm = ps_m0 if par == 0 else ps_m1
            md = None
            for dt_ in range(DT):
                md = pr.op("tensor",
                      lambda e, dt_=dt_, buf=buf, psm=psm: e.matmul(
                          psm[0:1, 0:MSW], silcb[:, dt_:dt_ + 1],
                          buf[:, dt_ * MSW:(dt_ + 1) * MSW],
                          start=(dt_ == 0), stop=(dt_ == DT - 1)),
                      waits=([(dsem, cnt), (sV, silb_done), (sV, ev_par[par])]
                             if dt_ == 0 else []),
                      inc=(sT if dt_ == DT - 1 else None))
            mv_done.append(md)
            ev_par[par] = pr.op("vector", lambda e, ms=ms, psm=psm: e.tensor_copy(
                out=modrow[0:1, ms * MSW:(ms + 1) * MSW], in_=psm[0:1, 0:MSW]),
                waits=[(sT, md)], inc=sV)
        mod_done = sV.n

        col_specs = [(0, sacol, 0.0), (1, cacol, 1.0), (3, sbcol, 0.0), (4, cbcol, 1.0)]
        cols_mm = None
        for vi, (seg, dstcol, addone) in enumerate(col_specs):
            for t in range(DT):
                cols_mm = pr.op("tensor", lambda e, seg=seg, t=t, vi=vi: e.matmul(
                    ps_col[:, vi * 16 + t:vi * 16 + t + 1],
                    modrow[0:1, seg * D + t * P: seg * D + (t + 1) * P],
                    ones_f[0:1, 0:1], start=True, stop=True),
                    waits=[(sV, mod_done)] if (vi == 0 and t == 0) else [],
                    inc=(sT if (vi == 3 and t == DT - 1) else None))
        for vi, (seg, dstcol, addone) in enumerate(col_specs):
            pr.op("vector", lambda e, vi=vi, dstcol=dstcol, addone=addone: e.tensor_scalar(
                out=dstcol[:, :], in0=ps_col[:, vi * 16:(vi + 1) * 16],
                scalar1=float(addone), scalar2=None, op0=ALU.add),
                waits=[(sT, cols_mm)], inc=sV)

        gprev = 0
        gprev_s = 0
        for seg, dst in ((2, garow), (5, gbrow)):
            src = modrow[0:1, seg * D:(seg + 1) * D]
            g1 = pr.op("scalar", lambda e, src=src: e.activation(
                out=grow[0:1, :], in_=src, func=AF.Sign),
                waits=[(sV, mod_done), (sV, gprev), (sS, gprev_s)], inc=sS)
            g2 = pr.op("scalar", lambda e, src=src: e.activation(
                out=grow2[0:1, :], in_=src, func=AF.Abs),
                waits=[(sS, g1)], inc=sS)
            gprev_s = g2
            g3 = pr.op("vector", lambda e: e.tensor_scalar_max(
                out=grow2[0:1, :], in0=grow2[0:1, :], scalar1=0.25),
                waits=[(sS, g2)], inc=sV)
            gprev = pr.op("vector", lambda e, dst=dst: e.tensor_tensor(
                out=dst[0:1, :], in0=grow[0:1, :], in1=grow2[0:1, :], op=ALU.mult),
                waits=[(sV, g3)], inc=sV)
        if debug:
            dbg["modrow"] = nc.dram_tensor("dbg_modrow", [1, MS], F32,
                                           kind="ExternalOutput").ap()
            dma("gpsimd", dbg["modrow"], modrow[0:1, :], dOut, waits=[(sV, sV.n)])
        BARRIER()
        pr.flush()
    _sc1.close()

    h1T = es.enter_context(nc.sbuf_tensor("h1T", [P, DT * D], BF16, side="left"))

    _sc2 = ExitStack()
    xr0 = _sc2.enter_context(nc.sbuf_tensor("xr0", [P, D], F32, side="right"))
    xr1 = _sc2.enter_context(nc.sbuf_tensor("xr1", [P, D], F32, side="right"))
    junk2 = _sc2.enter_context(nc.sbuf_tensor("junk2", [P, D], BF16, side="right"))
    lnxb0 = _sc2.enter_context(nc.sbuf_tensor("lnxb0", [P, D], BF16, side="right"))
    lnxb1 = _sc2.enter_context(nc.sbuf_tensor("lnxb1", [P, D], BF16, side="right"))
    ps_t0 = _sc2.enter_context(nc.psum_tensor("ps_t0", [P, 1024], BF16))
    ps_t1 = _sc2.enter_context(nc.psum_tensor("ps_t1", [P, 1024], BF16))
    if True:

        xr = (xr0, xr1)
        xsem = (dA, dB)
        vdone = [0, 0]
        mean_done = pr.op("vector", lambda e: e.tensor_scalar(
            out=mean1[:, :], in0=msum[:, :], scalar1=1.0 / D, scalar2=None,
            op0=ALU.mult), waits=[(sV, sV.n), (sS, sS.n)], inc=sV)
        c1 = pr.op("vector", lambda e: e.tensor_scalar(
            out=tmp1[:, 0:TT], in0=sqsum[:, :], scalar1=1.0 / D, scalar2=None,
            op0=ALU.mult), waits=[(sV, mean_done)], inc=sV)
        c2 = pr.op("vector", lambda e: e.tensor_tensor(
            out=tmp1[:, 16:32], in0=mean1[:, :], in1=mean1[:, :], op=ALU.mult),
            waits=[(sV, c1)], inc=sV)
        vstat = pr.op("vector", lambda e: e.tensor_tensor(
            out=tmp1[:, 0:TT], in0=tmp1[:, 0:TT], in1=tmp1[:, 16:32],
            op=ALU.subtract), waits=[(sV, c2)], inc=sV)
        ln1d = pr.op("scalar", lambda e: e.activation(
            out=tmp1[:, 0:TT], in_=tmp1[:, 0:TT], func=AF.Ln, bias=LNEPS, scale=1.0),
            waits=[(sV, vstat)], inc=sS)
        rstd_done = pr.op("scalar", lambda e: e.activation(
            out=rstd1[:, :], in_=tmp1[:, 0:TT], func=AF.Exp, scale=-0.5),
            waits=[(sS, ln1d)], inc=sS)

        tile_tr_done = [0, 0]
        ev_done = [0, 0]
        for t in range(TT):
            par = t % 2
            lnxb = lnxb0 if par == 0 else lnxb1
            w = [(sV, vdone[par]), (sS, rstd_done)]
            cnt = dma("sync", xr[par][:, :], x_d[t], xsem[par], waits=w)
            vdone[par] = pr.op("vector", lambda e, par=par, t=t, lnxb=lnxb: e.tensor_scalar(
                out=lnxb[:, :], in0=xr[par][:, :],
                scalar1=mean1[:, t:t + 1], scalar2=rstd1[:, t:t + 1],
                op0=ALU.subtract, op1=ALU.mult),
                waits=[(xsem[par], cnt), (sS, rstd_done), (sV, mean_done),
                       (sT, tile_tr_done[par])], inc=sV)
            for dt_ in range(DT):
                ppar = dt_ % 2
                ptr = ps_t0 if ppar == 0 else ps_t1
                td = pr.op("tensor", lambda e, dt_=dt_, ptr=ptr, lnxb=lnxb: e.transpose(
                    ptr[:, 0:P], lnxb[:, dt_ * P:(dt_ + 1) * P], eye_bf[:, :]),
                    waits=[(sV, vdone[par]), (sS, ev_done[ppar])], inc=sT)
                ev_done[ppar] = pr.op("scalar", lambda e, dt_=dt_, t=t, ptr=ptr:
                                      e.activation(
                                          out=h1T[:, dt_ * D + t * P: dt_ * D + (t + 1) * P],
                                          in_=ptr[:, 0:P], func=AF.Identity,
                                          scale=cacol[:, dt_:dt_ + 1],
                                          bias=sacol[:, dt_:dt_ + 1]),
                                      waits=[(sT, td)], inc=sS)
            tile_tr_done[par] = sT.n
        BARRIER()
        if debug:
            dbg["h1T"] = nc.dram_tensor("dbg_h1T", [P, DT * D], BF16,
                                        kind="ExternalOutput").ap()
            dma("gpsimd", dbg["h1T"], h1T[:, :], dOut)
            BARRIER()
        pr.flush()
    _sc2.close()

    _sc3 = ExitStack()
    wfbuf = _sc3.enter_context(nc.sbuf_tensor("wfbuf", [P, DT * H], F32, side="right"))
    wfbf = _sc3.enter_context(nc.sbuf_tensor("wfbf", [P, DT * H], BF16, side="right"))
    ldT = _sc3.enter_context(nc.sbuf_tensor("ldT", [16, D], F32, side="right"))
    totbc = _sc3.enter_context(nc.sbuf_tensor("totbc", [P, NCH * H], F32, side="right"))
    ps_l0 = _sc3.enter_context(nc.psum_tensor("ps_l0", [16, 512], F32))
    ps_l1 = _sc3.enter_context(nc.psum_tensor("ps_l1", [16, 512], F32))
    ps_c0 = _sc3.enter_context(nc.psum_tensor("ps_c0", [P, 512], F32))
    ps_c1 = _sc3.enter_context(nc.psum_tensor("ps_c1", [P, 512], F32))
    if True:

        ps_c = (ps_c0, ps_c1)
        cnt = dma("sync", wfbuf[:, :].rearrange("p (dt h) -> p dt h", dt=DT), wf_d.rearrange("dt p h -> p dt h"), dA)
        wfc = pr.op("vector", lambda e: e.tensor_copy(out=wfbf[:, :], in_=wfbuf[:, :]),
                    waits=[(dA, cnt)], inc=sV)
        ld_ev = [0, 0]
        for ch in range(4):
            lp = ch % 2
            pld = ps_l0 if lp == 0 else ps_l1
            md = None
            for dt_ in range(DT):
                md = pr.op("tensor", lambda e, ch=ch, dt_=dt_, pld=pld: e.matmul(
                    pld[:, :], wfbf[:, dt_ * H:(dt_ + 1) * H],
                    h1T[:, dt_ * D + ch * 512: dt_ * D + (ch + 1) * 512],
                    start=(dt_ == 0), stop=(dt_ == DT - 1)),
                    waits=[(sV, wfc), (sS, ld_ev[lp])] if dt_ == 0 else [],
                    inc=(sT if dt_ == DT - 1 else None))
            ld_ev[lp] = pr.op("scalar", lambda e, ch=ch, pld=pld: e.activation(
                out=ldT[:, ch * 512:(ch + 1) * 512], in_=pld[:, :], func=AF.Sigmoid),
                waits=[(sT, md)], inc=sS)
        lds = pr.op("scalar", lambda e: e.activation(
            out=ldT[:, :], in_=ldT[:, :], func=AF.Ln),
            waits=[(sS, max(ld_ev))], inc=sS)

        pev = [0, 0]
        for c in range(NCH):
            cp = c % 2
            td = pr.op("tensor", lambda e, c=c, cp=cp: e.transpose(
                ps_c[cp][0:P, 0:16], ldT[:, c * P:(c + 1) * P], eye16[:, :]),
                waits=[(sS, lds), (sV, pev[cp])], inc=sT)
            pev[cp] = pr.op("vector", lambda e, c=c, cp=cp: e.tensor_copy(
                out=ldtok[:, c * H:(c + 1) * H], in_=ps_c[cp][0:P, 0:16]),
                waits=[(sT, td)], inc=sV)
        ld_all = sV.n
        for c in range(NCH):
            cp = c % 2
            pr.op("tensor", lambda e, c=c, cp=cp: e.matmul(
                ps_c[cp][:, 0:16], tri[:, :], ldtok[:, c * H:(c + 1) * H],
                start=True, stop=True),
                waits=[(sV, ld_all), (sV, pev[cp])])
            td = pr.op("tensor", lambda e, c=c, cp=cp: e.matmul(
                ps_c[cp][0:1, 16:32], onescol[:, :], ldtok[:, c * H:(c + 1) * H],
                start=True, stop=True), inc=sT)
            pr.op("vector", lambda e, c=c, cp=cp: e.tensor_copy(
                out=cumall[:, c * H:(c + 1) * H], in_=ps_c[cp][:, 0:16]),
                waits=[(sT, td)])
            pev[cp] = pr.op("vector", lambda e, c=c, cp=cp: e.tensor_copy(
                out=totrow[0:1, c * H:(c + 1) * H], in_=ps_c[cp][0:1, 16:32]),
                inc=sV)
        cum_done = sV.n

        nbc = NCH * H // P
        for i in range(nbc):
            cp = i % 2
            td = pr.op("tensor", lambda e, i=i, cp=cp: e.matmul(
                ps_c[cp][:, 0:P], ones_f[0:1, :], totrow[0:1, i * P:(i + 1) * P],
                start=True, stop=True), waits=[(sV, cum_done), (sV, pev[cp])], inc=sT)
            pev[cp] = pr.op("vector", lambda e, i=i, cp=cp: e.tensor_copy(
                out=totbc[:, i * P:(i + 1) * P], in_=ps_c[cp][:, 0:P]),
                waits=[(sT, td)], inc=sV)
        bc_ev = sV.n

        vt = pr.op("vector", lambda e: e.scalar_tensor_tensor(
            out=tmp1[:, :], in0=totbc[:, :], scalar=0.5, in1=cumall[:, :],
            op0=ALU.mult, op1=ALU.subtract), waits=[(sV, bc_ev)], inc=sV)
        pr.op("scalar", lambda e: e.activation(out=e1t[:, :], in_=tmp1[:, :],
                                               func=AF.Exp), waits=[(sV, vt)])
        e12 = pr.op("scalar", lambda e: e.activation(out=e2t[:, :], in_=tmp1[:, :],
                                                     func=AF.Exp, scale=-1.0), inc=sS)
        NE = (NCH - 1) * H
        p3 = pr.op("vector", lambda e: e.tensor_tensor(
            out=tmp1[:, 0:NE], in0=totbc[:, 0:NE], in1=cumall[:, 0:NE],
            op=ALU.subtract), waits=[(sS, e12)], inc=sV)
        vt3 = pr.op("vector", lambda e: e.scalar_tensor_tensor(
            out=tmp1[:, 0:NE], in0=totbc[:, H:H + NE], scalar=0.5, in1=tmp1[:, 0:NE],
            op0=ALU.mult, op1=ALU.add), waits=[(sV, p3)], inc=sV)
        e3d = pr.op("scalar", lambda e: e.activation(
            out=e3t[:, :], in_=tmp1[:, 0:NE], func=AF.Exp), waits=[(sV, vt3)], inc=sS)
        vt4 = pr.op("vector", lambda e: e.tensor_tensor(
            out=tmp1[0:1, 0:NE], in0=totrow[0:1, 0:NE],
            in1=totrow[0:1, H:H + NE], op=ALU.add), waits=[(sS, e3d)], inc=sV)
        e4r = pr.op("scalar", lambda e: e.activation(
            out=tmp1[0:1, 0:NE], in_=tmp1[0:1, 0:NE], func=AF.Exp, scale=0.5),
            waits=[(sV, vt4)], inc=sS)
        for i in range((NE + P - 1) // P):
            w_ = min(P, NE - i * P)
            cp = i % 2
            td = pr.op("tensor", lambda e, i=i, w_=w_, cp=cp: e.matmul(
                ps_c[cp][:, 0:w_], ones_f[0:1, :], tmp1[0:1, i * P:i * P + w_],
                start=True, stop=True), waits=[(sS, e4r), (sV, pev[cp])], inc=sT)
            pev[cp] = pr.op("vector", lambda e, i=i, w_=w_, cp=cp: e.tensor_copy(
                out=e4bc[:, i * P:i * P + w_], in_=ps_c[cp][:, 0:w_]),
                waits=[(sT, td)], inc=sV)
        pr.op("vector", lambda e: e.memset(ssq[:, :], 0.0), inc=sV)
        BARRIER()
        if debug:
            for nm, tens, wdt in (("ldtok", ldtok, NCH * H), ("cumall", cumall, NCH * H),
                                  ("e1t", e1t, NCH * H), ("e2t", e2t, NCH * H),
                                  ("e3t", e3t, NE), ("e4bc", e4bc, NE)):
                dbg[nm] = nc.dram_tensor("dbg_" + nm, [P, wdt], F32,
                                         kind="ExternalOutput").ap()
                dma("gpsimd", dbg[nm], tens[:, :], dOut)
            BARRIER()
        pr.flush()
    _sc3.close()

    _sc4 = ExitStack()
    wbuf = _sc4.enter_context(nc.sbuf_tensor("wbuf", [P, 3 * D], F32, side="right"))
    wqbf = _sc4.enter_context(nc.sbuf_tensor("wqbf", [P, D], BF16, side="right"))
    wkbf = _sc4.enter_context(nc.sbuf_tensor("wkbf", [P, D], BF16, side="right"))
    wvbf = _sc4.enter_context(nc.sbuf_tensor("wvbf", [P, D], BF16, side="right"))
    qT = _sc4.enter_context(nc.sbuf_tensor("qT", [P, D], BF16, side="right"))
    kT = _sc4.enter_context(nc.sbuf_tensor("kT", [P, D], BF16, side="right"))
    vT = _sc4.enter_context(nc.sbuf_tensor("vT", [P, D], BF16, side="right"))
    ktok = _sc4.enter_context(nc.sbuf_tensor("ktok", [P, D], BF16, side="right"))
    vtok = _sc4.enter_context(nc.sbuf_tensor("vtok", [P, D], BF16, side="right"))
    s3b0 = _sc4.enter_context(nc.sbuf_tensor("s3b0", [P, P], BF16, side="right"))
    s3b1 = _sc4.enter_context(nc.sbuf_tensor("s3b1", [P, P], BF16, side="right"))
    ob0 = _sc4.enter_context(nc.sbuf_tensor("ob0", [P, P], BF16, side="right"))
    ob1 = _sc4.enter_context(nc.sbuf_tensor("ob1", [P, P], BF16, side="right"))
    kd0 = _sc4.enter_context(nc.sbuf_tensor("kd0", [P, P], BF16, side="right"))
    kd1 = _sc4.enter_context(nc.sbuf_tensor("kd1", [P, P], BF16, side="right"))
    junk5 = _sc4.enter_context(nc.sbuf_tensor("junk5", [P, P], BF16, side="right"))
    ps_p0 = _sc4.enter_context(nc.psum_tensor("ps_p0", [P, 512], F32))
    ps_p1 = _sc4.enter_context(nc.psum_tensor("ps_p1", [P, 512], F32))
    ps_s = _sc4.enter_context(nc.psum_tensor("ps_s", [P, 512], F32))
    ps_o0 = _sc4.enter_context(nc.psum_tensor("ps_o0", [P, 512], F32))
    ps_o1 = _sc4.enter_context(nc.psum_tensor("ps_o1", [P, 512], F32))
    ps_S = _sc4.enter_context(nc.psum_tensor("ps_S", [P, 512], F32))
    ps_x0 = _sc4.enter_context(nc.psum_tensor("ps_x0", [P, 1024], BF16))
    ps_x1 = _sc4.enter_context(nc.psum_tensor("ps_x1", [P, 1024], BF16))
    if True:

        wsem = (dA, dB)
        wcnt = [0, 0]
        proj_ev = [0, 0]
        tr_ev = [0, 0]
        last_m2m3 = 0
        mat_mm_done = [0, 0, 0]
        cast_done = 0
        cast_last = 0

        wcnt[0] = dma("sync", wbuf[:, :].rearrange("p (m dt e) -> p m dt e", m=3, dt=DT),
                      wqkv_d[0].rearrange("m dt p e -> p m dt e"), wsem[0])
        pr.op("vector", lambda e: e.tensor_copy(out=wqbf[:, :], in_=wbuf[:, 0:D]),
              waits=[(wsem[0], wcnt[0])])
        pr.op("vector", lambda e: e.tensor_copy(out=wkbf[:, :], in_=wbuf[:, D:2 * D]))
        cast_done = pr.op("vector", lambda e: e.tensor_copy(
            out=wvbf[:, :], in_=wbuf[:, 2 * D:3 * D]), inc=sV)

        for h in range(H):
            par = h % 2

            def proj(wbf, dstT, func, mat):
                lastmm = 0
                for ch in range(4):
                    ppar = ch % 2
                    pp = ps_p0 if ppar == 0 else ps_p1
                    md = None
                    for dt_ in range(DT):
                        md = pr.op("tensor", lambda e, ch=ch, dt_=dt_, pp=pp, wbf=wbf:
                              e.matmul(pp[:, :], wbf[:, dt_ * P:(dt_ + 1) * P],
                                       h1T[:, dt_ * D + ch * 512: dt_ * D + (ch + 1) * 512],
                                       start=(dt_ == 0), stop=(dt_ == DT - 1)),
                              waits=([(sV, cast_done), (sS, proj_ev[ppar])]
                                     if dt_ == 0 else []),
                              inc=(sT if dt_ == DT - 1 else None))
                    lastmm = md
                    proj_ev[ppar] = pr.op(
                        "scalar",
                        lambda e, ch=ch, pp=pp, dstT=dstT, func=func: e.activation(
                            out=dstT[:, ch * 512:(ch + 1) * 512], in_=pp[:, :],
                            func=func),
                        waits=[(sT, md)], inc=sS)
                mat_mm_done[mat] = lastmm
                return sS.n

            qd = proj(wqbf, qT, AF.Silu, 0)
            kdn = proj(wkbf, kT, AF.Silu, 1)
            vd = proj(wvbf, vT, AF.Copy, 2)

            if h + 1 < H:
                wcnt[1 - par] = dma("sync",
                                    wbuf[:, :].rearrange("p (m dt e) -> p m dt e",
                                                         m=3, dt=DT),
                                    wqkv_d[h + 1].rearrange("m dt p e -> p m dt e"),
                                    wsem[1 - par],
                                    waits=[(sV, cast_last if h > 0 else cast_done)])
                pr.op("vector", lambda e: e.tensor_copy(
                    out=wqbf[:, :], in_=wbuf[:, 0:D]),
                    waits=[(wsem[1 - par], wcnt[1 - par]), (sT, mat_mm_done[0])])
                pr.op("vector", lambda e: e.tensor_copy(
                    out=wkbf[:, :], in_=wbuf[:, D:2 * D]),
                    waits=[(sT, mat_mm_done[1])])
                cast_last = pr.op("vector", lambda e: e.tensor_copy(
                    out=wvbf[:, :], in_=wbuf[:, 2 * D:3 * D]),
                    waits=[(sT, mat_mm_done[2])], inc=sV)
            else:
                cast_last = cast_done
            next_cast_done = cast_last

            def transp_all(srcT, dst, sdone):
                last = 0
                for c in range(NCH):
                    ppar = c % 2
                    pt = ps_x0 if ppar == 0 else ps_x1
                    td = pr.op("tensor", lambda e, c=c, pt=pt, srcT=srcT: e.transpose(
                        pt[:, 0:P], srcT[:, c * P:(c + 1) * P], eye_bf[:, :]),
                        waits=[(sS, sdone), (sV, tr_ev[ppar])], inc=sT)
                    tr_ev[ppar] = pr.op("vector",
                                        lambda e, c=c, pt=pt, dst=dst: e.tensor_copy(
                                            out=dst[:, c * P:(c + 1) * P],
                                            in_=pt[:, 0:P]),
                                        waits=[(sT, td)], inc=sV)
                    last = tr_ev[ppar]
                return last

            ktd = transp_all(kT, ktok, kdn)
            vtd = transp_all(vT, vtok, vd)

            supd = pr.op("vector", lambda e: e.memset(Sbf[:, :], 0.0),
                         waits=[(sT, last_m2m3)], inc=sV)
            m2m3_done = 0
            m4_done = 0
            oev = [0, 0]
            s1ev = 0
            s3v = [0, 0]
            kdv = [0, 0]
            otr = [0, 0]
            for c in range(NCH):
                cp = c % 2
                col = c * H + h
                s3buf = s3b0 if cp == 0 else s3b1
                obuf = ob0 if cp == 0 else ob1
                kdbuf = kd0 if cp == 0 else kd1
                po = ps_o0 if cp == 0 else ps_o1
                pt = ps_x0 if cp == 0 else ps_x1

                if c < NCH - 1:
                    kdv[cp] = pr.op("scalar", lambda e, c=c, h=h, kdbuf=kdbuf:
                                    e.activation(out=kdbuf[:, :],
                                                 in_=ktok[:, c * P:(c + 1) * P],
                                                 func=AF.Identity,
                                                 scale=e3t[:, c * H + h:c * H + h + 1]),
                                    waits=[(sV, ktd), (sT, m4_done)], inc=sS)

                m1 = pr.op("tensor", lambda e, c=c: e.matmul(
                    ps_s[:, 0:P], kT[:, c * P:(c + 1) * P], qT[:, c * P:(c + 1) * P],
                    start=True, stop=True),
                    waits=[(sS, s1ev), (sV, vtd)], inc=sT)
                s1ev = pr.op("scalar", lambda e, col=col, s3buf=s3buf: e.activation(
                    out=s3buf[:, :], in_=ps_s[:, 0:P], func=AF.Identity,
                    scale=e1t[:, col:col + 1]),
                    waits=[(sT, m1), (sT, m2m3_done)], inc=sS)
                s3v[cp] = pr.op("vector", lambda e, s3buf=s3buf: e.tensor_tensor(
                    out=s3buf[:, :], in0=s3buf[:, :], in1=maskb[:, :], op=ALU.mult),
                    waits=[(sS, s1ev)], inc=sV)

                m2m3_done = pr.op("tensor", lambda e, c=c, po=po, s3buf=s3buf: e.matmul(
                    po[:, 0:P], s3buf[:, :], vtok[:, c * P:(c + 1) * P],
                    start=True, stop=(c == 0)),
                    waits=[(sV, s3v[cp]), (sS, oev[cp])],
                    inc=(sT if c == 0 else None))
                if c > 0:
                    m2m3_done = pr.op("tensor", lambda e, c=c, po=po: e.matmul(
                        po[:, 0:P], qT[:, c * P:(c + 1) * P], Sbf[:, :],
                        start=False, stop=True),
                        waits=[(sV, supd)], inc=sT)
                last_m2m3 = m2m3_done

                if c < NCH - 1:
                    m4_done = pr.op("tensor", lambda e, c=c, kdbuf=kdbuf: e.matmul(
                        ps_S[:, 0:P], kdbuf[:, :], vtok[:, c * P:(c + 1) * P],
                        start=True, stop=True),
                        waits=[(sS, kdv[cp]), (sV, supd)], inc=sT)
                    supd = pr.op("vector", lambda e, c=c, h=h: e.scalar_tensor_tensor(
                        out=Sbf[:, :], in0=Sbf[:, :],
                        scalar=e4bc[:, c * H + h:c * H + h + 1],
                        in1=ps_S[:, 0:P], op0=ALU.mult, op1=ALU.add),
                        waits=[(sT, m4_done), (sT, m2m3_done), (sV, supd)], inc=sV)

                oev[cp] = pr.op("scalar", lambda e, col=col, po=po, obuf=obuf:
                                e.activation(out=obuf[:, :], in_=po[:, 0:P],
                                             func=AF.Identity, scale=e2t[:, col:col + 1]),
                                waits=[(sT, m2m3_done), (sV, otr[cp])], inc=sS)
                sqv = pr.op("vector", lambda e, obuf=obuf: e.scalar_tensor_tensor(
                    out=junk5[:, :], in0=obuf[:, :], scalar=1.0, in1=obuf[:, :],
                    op0=ALU.mult, op1=ALU.mult, accum_out=ssqt[:, :]),
                    waits=[(sS, oev[cp])], inc=sV)
                pr.op("vector", lambda e, c=c: e.tensor_tensor(
                    out=ssq[:, c:c + 1], in0=ssq[:, c:c + 1], in1=ssqt[:, :],
                    op=ALU.add), waits=[(sV, sqv)], inc=sV)
                otd = pr.op("tensor", lambda e, obuf=obuf, pt=pt: e.transpose(
                    pt[:, 0:P], obuf[:, :], eye_bf[:, :]),
                    waits=[(sS, oev[cp]), (sV, tr_ev[cp])], inc=sT)
                tr_ev[cp] = pr.op("vector", lambda e, c=c, h=h, pt=pt: e.tensor_copy(
                    out=oT[:, h * D + c * P: h * D + (c + 1) * P], in_=pt[:, 0:P]),
                    waits=[(sT, otd)], inc=sV)
                otr[cp] = tr_ev[cp]
            cast_done = next_cast_done
        BARRIER()
        if debug:
            for nm, tens in (("qT", qT), ("kT", kT), ("vtok", vtok), ("ktok", ktok)):
                dbg[nm] = nc.dram_tensor("dbg_" + nm, [P, D], BF16,
                                         kind="ExternalOutput").ap()
                dma("gpsimd", dbg[nm], tens[:, :], dOut)
            dbg["oT"] = nc.dram_tensor("dbg_oT", [P, H * D], BF16,
                                       kind="ExternalOutput").ap()
            dbg["ssq"] = nc.dram_tensor("dbg_ssq", [P, TT], F32,
                                        kind="ExternalOutput").ap()
            dma("gpsimd", dbg["oT"], oT[:, :], dOut)
            dma("gpsimd", dbg["ssq"], ssq[:, :], dOut)
            BARRIER()
        pr.flush()
    _sc4.close()

    wo_bf = h1T

    lnr = pr.op("scalar", lambda e: e.activation(
        out=tmp1[:, 0:TT], in_=ssq[:, :], func=AF.Ln, bias=RMSEPS, scale=1.0 / D),
        inc=sS)
    rms_done = pr.op("scalar", lambda e: e.activation(
        out=rmsr[:, :], in_=tmp1[:, 0:TT], func=AF.Exp, scale=-0.5),
        waits=[(sS, lnr)], inc=sS)

    _sc5 = ExitStack()
    wob0 = _sc5.enter_context(nc.sbuf_tensor("wobuf0", [P, D], F32, side="right"))
    wob1 = _sc5.enter_context(nc.sbuf_tensor("wobuf1", [P, D], F32, side="right"))
    xrc0 = _sc5.enter_context(nc.sbuf_tensor("xrc0", [P, D], F32, side="right"))
    xrc1 = _sc5.enter_context(nc.sbuf_tensor("xrc1", [P, D], F32, side="right"))
    x1b0 = _sc5.enter_context(nc.sbuf_tensor("x1b0", [P, D], F32, side="right"))
    x1b1 = _sc5.enter_context(nc.sbuf_tensor("x1b1", [P, D], F32, side="right"))
    gaB = _sc5.enter_context(nc.sbuf_tensor("gaB", [P, D], BF16, side="right"))
    junk6 = _sc5.enter_context(nc.sbuf_tensor("junk6", [P, D], BF16, side="right"))
    ps_a0 = _sc5.enter_context(nc.psum_tensor("ps_a0", [P, 512], F32))
    ps_a1 = _sc5.enter_context(nc.psum_tensor("ps_a1", [P, 512], F32))
    ps_a2 = _sc5.enter_context(nc.psum_tensor("ps_a2", [P, 512], F32))
    ps_a3 = _sc5.enter_context(nc.psum_tensor("ps_a3", [P, 512], F32))
    ps_gb = _sc5.enter_context(nc.psum_tensor("ps_gb", [P, 512], F32))
    if True:

        ps_a = (ps_a0, ps_a1, ps_a2, ps_a3)
        gaB_done = 0
        gev = 0
        for i in range(4):
            td = pr.op("tensor", lambda e, i=i: e.matmul(
                ps_gb[:, :], ones_b[0:1, :], garow[0:1, i * 512:(i + 1) * 512],
                start=True, stop=True), waits=[(sV, gev)], inc=sT)
            gev = pr.op("vector", lambda e, i=i: e.tensor_copy(
                out=gaB[:, i * 512:(i + 1) * 512], in_=ps_gb[:, :]),
                waits=[(sT, td)], inc=sV)
        gaB_done = gev

        wsem = (dA, dB)
        wob = (wob0, wob1)
        wo_cast = [0] * DT
        ccount = [0, 0]
        for ft in range(DT):
            par = ft % 2
            w = [(sS, ccount[par])] if ft >= 2 else []
            cnt = dma("sync", wob[par][:, :], wo_d[ft], wsem[par], waits=w)
            ccount[par] = pr.op("scalar", lambda e, ft=ft, par=par: e.activation(
                out=wo_bf[:, ft * D:(ft + 1) * D], in_=wob[par][:, :], func=AF.Identity,
                scale=nwcol[:, ft:ft + 1]),
                waits=[(wsem[par], cnt)], inc=sS)
            wo_cast[ft] = ccount[par]

        xsem6 = (dC, dE)
        x1sem = (dF, dG)
        xcnt = [0, 0]
        x1w = [0, 0]
        vread = [0, 0]
        evs = [0, 0, 0, 0]
        sst = 0
        for t in range(TT):
            par = t % 2
            xrc = xrc0 if par == 0 else xrc1
            x1buf = x1b0 if par == 0 else x1b1
            w = [(sV, vread[par])] if t >= 2 else []
            xcnt[par] = dma("sync", xrc[:, :], x_d[t], xsem6[par], waits=w)
            mmd = None
            for ft in range(DT):
                for mc in range(4):
                    mmd = pr.op("tensor", lambda e, t=t, mc=mc, ft=ft: e.matmul(
                        ps_a[mc][:, :], oT[:, ft * D + t * P:ft * D + (t + 1) * P],
                        wo_bf[:, ft * D + mc * 512: ft * D + (mc + 1) * 512],
                        start=(ft == 0), stop=(ft == DT - 1)),
                        waits=([(sS, wo_cast[ft])] if t == 0 and mc == 0 else []) +
                              ([(sV, evs[mc])] if ft == 0 else []),
                        inc=(sT if (ft == DT - 1 and mc == 3) else None))
            for mc in range(4):
                sttd = pr.op("vector", lambda e, t=t, mc=mc, x1buf=x1buf:
                             e.scalar_tensor_tensor(
                                 out=x1buf[:, mc * 512:(mc + 1) * 512],
                                 in0=ps_a[mc][:, :],
                                 scalar=rmsr[:, t:t + 1],
                                 in1=gaB[:, mc * 512:(mc + 1) * 512],
                                 op0=ALU.mult, op1=ALU.mult),
                             waits=[(sT, mmd), (sS, rms_done), (sV, gaB_done),
                                    (x1sem[par], x1w[par])], inc=sV)
                evs[mc] = pr.op("vector", lambda e, mc=mc, x1buf=x1buf, xrc=xrc:
                                e.tensor_tensor(
                                    out=x1buf[:, mc * 512:(mc + 1) * 512],
                                    in0=x1buf[:, mc * 512:(mc + 1) * 512],
                                    in1=xrc[:, mc * 512:(mc + 1) * 512],
                                    op=ALU.add),
                                waits=[(xsem6[par], xcnt[par]), (sV, sttd)], inc=sV)
            vread[par] = evs[3]
            vst = pr.op("vector", lambda e, t=t, x1buf=x1buf: e.tensor_reduce(
                out=msum[:, t:t + 1], in_=x1buf[:, :], axis=AX.X, op=ALU.add),
                waits=[(sV, evs[3])], inc=sV)
            sst = pr.op("scalar", lambda e, t=t, x1buf=x1buf: e.activation(
                out=junk6[:, :], in_=x1buf[:, :], func=AF.Square,
                accum_out=sqsum[:, t:t + 1]),
                waits=[(sV, evs[3]), (sS, sst if t else 0)], inc=sS)
            x1w[par] = dma("gpsimd", x1_hbm[t], x1buf[:, :], x1sem[par],
                           waits=[(sV, vst), (sS, sst)])

        mean2_done = pr.op("vector", lambda e: e.tensor_scalar(
            out=mean2[:, :], in0=msum[:, :], scalar1=1.0 / D, scalar2=None,
            op0=ALU.mult), waits=[(sV, sV.n), (sS, sS.n)], inc=sV)
        c1b = pr.op("vector", lambda e: e.tensor_scalar(
            out=tmp1[:, 0:TT], in0=sqsum[:, :], scalar1=1.0 / D, scalar2=None,
            op0=ALU.mult), waits=[(sV, mean2_done)], inc=sV)
        c2b = pr.op("vector", lambda e: e.tensor_tensor(
            out=tmp1[:, 16:32], in0=mean2[:, :], in1=mean2[:, :], op=ALU.mult),
            waits=[(sV, c1b)], inc=sV)
        vst2 = pr.op("vector", lambda e: e.tensor_tensor(
            out=tmp1[:, 0:TT], in0=tmp1[:, 0:TT], in1=tmp1[:, 16:32],
            op=ALU.subtract), waits=[(sV, c2b)], inc=sV)
        ln2d = pr.op("scalar", lambda e: e.activation(
            out=tmp1[:, 0:TT], in_=tmp1[:, 0:TT], func=AF.Ln, bias=LNEPS, scale=1.0),
            waits=[(sV, vst2)], inc=sS)
        pr.op("scalar", lambda e: e.activation(
            out=rstd2[:, :], in_=tmp1[:, 0:TT], func=AF.Exp, scale=-0.5),
            waits=[(sS, ln2d)], inc=sS)
        BARRIER()
        if debug:
            dbg["x1"] = nc.dram_tensor("dbg_x1", [TT, P, D], F32,
                                       kind="ExternalOutput").ap()
            for t in range(TT):
                dma("gpsimd", dbg["x1"][t], x1_hbm[t], dOut)
            BARRIER()
        pr.flush()
    _sc5.close()

    h2T = h1T

    _sc6 = ExitStack()
    x1r0 = _sc6.enter_context(nc.sbuf_tensor("x1r0", [P, D], F32, side="right"))
    x1r1 = _sc6.enter_context(nc.sbuf_tensor("x1r1", [P, D], F32, side="right"))
    lnxb20 = _sc6.enter_context(nc.sbuf_tensor("lnxb20", [P, D], BF16, side="right"))
    lnxb21 = _sc6.enter_context(nc.sbuf_tensor("lnxb21", [P, D], BF16, side="right"))
    ps_u0 = _sc6.enter_context(nc.psum_tensor("ps_u0", [P, 1024], BF16))
    ps_u1 = _sc6.enter_context(nc.psum_tensor("ps_u1", [P, 1024], BF16))
    if True:

        x1r = (x1r0, x1r1)
        xsem7 = (dA, dB)
        vdone = [0, 0]
        tile_tr_done = [0, 0]
        ev_done = [0, 0]
        for t in range(TT):
            par = t % 2
            lnxb2 = lnxb20 if par == 0 else lnxb21
            w = [(sV, vdone[par])] if t >= 2 else []
            cnt = dma("sync", x1r[par][:, :], x1_hbm[t], xsem7[par], waits=w)
            vdone[par] = pr.op("vector", lambda e, par=par, t=t, lnxb2=lnxb2: e.tensor_scalar(
                out=lnxb2[:, :], in0=x1r[par][:, :],
                scalar1=mean2[:, t:t + 1], scalar2=rstd2[:, t:t + 1],
                op0=ALU.subtract, op1=ALU.mult),
                waits=[(xsem7[par], cnt), (sS, sS.n), (sV, mean2_done),
                       (sT, tile_tr_done[par])], inc=sV)
            for dt_ in range(DT):
                ppar = dt_ % 2
                ptr = ps_u0 if ppar == 0 else ps_u1
                td = pr.op("tensor", lambda e, dt_=dt_, ptr=ptr, lnxb2=lnxb2: e.transpose(
                    ptr[:, 0:P], lnxb2[:, dt_ * P:(dt_ + 1) * P], eye_bf[:, :]),
                    waits=[(sV, vdone[par]), (sS, ev_done[ppar])], inc=sT)
                ev_done[ppar] = pr.op("scalar", lambda e, dt_=dt_, t=t, ptr=ptr:
                                      e.activation(
                                          out=h2T[:, dt_ * D + t * P: dt_ * D + (t + 1) * P],
                                          in_=ptr[:, 0:P], func=AF.Identity,
                                          scale=cbcol[:, dt_:dt_ + 1],
                                          bias=sbcol[:, dt_:dt_ + 1]),
                                      waits=[(sT, td)], inc=sS)
            tile_tr_done[par] = sT.n
        BARRIER()
        if debug:
            dbg["h2T"] = nc.dram_tensor("dbg_h2T", [P, DT * D], BF16,
                                        kind="ExternalOutput").ap()
            dma("gpsimd", dbg["h2T"], h2T[:, :], dOut)
            BARRIER()
        pr.flush()
    _sc6.close()

    _sc7 = ExitStack()
    wgb0 = _sc7.enter_context(nc.sbuf_tensor("wgbuf0", [P, 2 * D], F32, side="right"))
    wgb1 = _sc7.enter_context(nc.sbuf_tensor("wgbuf1", [P, 2 * D], F32, side="right"))
    wg0bf = _sc7.enter_context(nc.sbuf_tensor("wg0bf", [P, D], BF16, side="right"))
    wg1bf = _sc7.enter_context(nc.sbuf_tensor("wg1bf", [P, D], BF16, side="right"))
    gelb0 = _sc7.enter_context(nc.sbuf_tensor("gelb0", [P, 512], BF16, side="right"))
    gelb1 = _sc7.enter_context(nc.sbuf_tensor("gelb1", [P, 512], BF16, side="right"))
    gelb2 = _sc7.enter_context(nc.sbuf_tensor("gelb2", [P, 512], BF16, side="right"))
    gelb3 = _sc7.enter_context(nc.sbuf_tensor("gelb3", [P, 512], BF16, side="right"))
    actb0 = _sc7.enter_context(nc.sbuf_tensor("actb0", [P, 512], BF16, side="right"))
    actb1 = _sc7.enter_context(nc.sbuf_tensor("actb1", [P, 512], BF16, side="right"))
    actb2 = _sc7.enter_context(nc.sbuf_tensor("actb2", [P, 512], BF16, side="right"))
    actb3 = _sc7.enter_context(nc.sbuf_tensor("actb3", [P, 512], BF16, side="right"))
    ps_g0 = _sc7.enter_context(nc.psum_tensor("ps_g0", [P, 512], F32))
    ps_g1 = _sc7.enter_context(nc.psum_tensor("ps_g1", [P, 512], F32))
    ps_g2 = _sc7.enter_context(nc.psum_tensor("ps_g2", [P, 512], F32))
    ps_g3 = _sc7.enter_context(nc.psum_tensor("ps_g3", [P, 512], F32))
    ps_f0 = _sc7.enter_context(nc.psum_tensor("ps_f0", [P, 512], F32))
    ps_f1 = _sc7.enter_context(nc.psum_tensor("ps_f1", [P, 512], F32))
    ps_f2 = _sc7.enter_context(nc.psum_tensor("ps_f2", [P, 512], F32))
    ps_f3 = _sc7.enter_context(nc.psum_tensor("ps_f3", [P, 512], F32))
    if True:

        wsem8 = (dA, dB)
        actsem = (dF, dG, dH, dI)
        wgb = (wgb0, wgb1)
        wcnt8 = [0, 0]
        gmm = 0
        fmm = 0
        gel_done = [0, 0, 0, 0]
        act_done = [0, 0, 0, 0]
        actw = [0, 0, 0, 0]
        ps_gs = (ps_g0, ps_g1, ps_g2, ps_g3)
        ps_fs = (ps_f0, ps_f1, ps_f2, ps_f3)
        gelbs = (gelb0, gelb1, gelb2, gelb3)
        actbs = (actb0, actb1, actb2, actb3)
        wcnt8[0] = dma("sync", wgb[0][:, :].rearrange("p (m dt e) -> p m dt e",
                                                      m=2, dt=DT),
                       wg_d[0].rearrange("m dt p e -> p m dt e"), wsem8[0])
        cast0 = cast1 = 0
        cast0 = pr.op("vector", lambda e: e.tensor_copy(
            out=wg0bf[:, :], in_=wgb[0][:, 0:D]),
            waits=[(wsem8[0], wcnt8[0])], inc=sV)
        cast1 = pr.op("vector", lambda e: e.tensor_copy(
            out=wg1bf[:, :], in_=wgb[0][:, D:2 * D]), inc=sV)
        wcnt8[1] = dma("sync", wgb[1][:, :].rearrange("p (m dt e) -> p m dt e",
                                                      m=2, dt=DT),
                       wg_d[1].rearrange("m dt p e -> p m dt e"), wsem8[1],
                       waits=[(sV, cast1)])
        for s in range(NS):
            par = s % 2
            ccur0, ccur1 = cast0, cast1
            for dt_ in range(DT):
                for tc in range(4):
                    gmm = pr.op("tensor", lambda e, tc=tc, dt_=dt_: e.matmul(
                        ps_gs[tc][:, :], wg0bf[:, dt_ * P:(dt_ + 1) * P],
                        h2T[:, dt_ * D + tc * 512: dt_ * D + (tc + 1) * 512],
                        start=(dt_ == 0), stop=(dt_ == DT - 1)),
                        waits=(([(sV, ccur0)] + [(sS, g) for g in gel_done])
                               if (dt_ == 0 and tc == 0) else []),
                        inc=(sT if (dt_ == DT - 1 and tc == 3) else None))
            for tc in range(4):
                gel_done[tc] = pr.op("scalar", lambda e, tc=tc: e.activation(
                    out=gelbs[tc][:, :], in_=ps_gs[tc][:, :],
                    func=AF.Gelu_apprx_tanh),
                    waits=[(sT, gmm), (sV, act_done[tc])], inc=sS)
            if s + 1 < NS:
                cast0 = pr.op("vector", lambda e, par=par: e.tensor_copy(
                    out=wg0bf[:, :], in_=wgb[1 - par][:, 0:D]),
                    waits=[(wsem8[1 - par], wcnt8[1 - par]), (sT, gmm)], inc=sV)
            for dt_ in range(DT):
                for tc in range(4):
                    fmm = pr.op("tensor", lambda e, tc=tc, dt_=dt_: e.matmul(
                        ps_fs[tc][:, :], wg1bf[:, dt_ * P:(dt_ + 1) * P],
                        h2T[:, dt_ * D + tc * 512: dt_ * D + (tc + 1) * 512],
                        start=(dt_ == 0), stop=(dt_ == DT - 1)),
                        waits=(([(sV, ccur1)] + [(sV, a) for a in act_done])
                               if (dt_ == 0 and tc == 0) else []),
                        inc=(sT if (dt_ == DT - 1 and tc == 3) else None))
            if s + 1 < NS:
                cast1 = pr.op("vector", lambda e, par=par: e.tensor_copy(
                    out=wg1bf[:, :], in_=wgb[1 - par][:, D:2 * D]),
                    waits=[(sT, fmm)], inc=sV)
                if s + 2 < NS:
                    wcnt8[par] = dma("sync",
                                     wgb[par][:, :].rearrange(
                                         "p (m dt e) -> p m dt e", m=2, dt=DT),
                                     wg_d[s + 2].rearrange("m dt p e -> p m dt e"),
                                     wsem8[par], waits=[(sV, cast1)])
            for tc in range(4):
                act_done[tc] = pr.op("vector", lambda e, tc=tc: e.tensor_tensor(
                    out=actbs[tc][:, :], in0=gelbs[tc][:, :], in1=ps_fs[tc][:, :],
                    op=ALU.mult),
                    waits=[(sS, gel_done[tc]), (sT, fmm),
                           (actsem[tc], actw[tc])], inc=sV)
                actw[tc] = dma("gpsimd", act_hbm[tc][s], actbs[tc][:, :],
                               actsem[tc], waits=[(sV, act_done[tc])])
        BARRIER()
        pr.flush()
    _sc7.close()

    _sc8 = ExitStack()
    wlbf = h1T
    wlb0 = _sc8.enter_context(nc.sbuf_tensor("wlbuf0", [P, 8 * QW], F32, side="right"))
    wlb1 = _sc8.enter_context(nc.sbuf_tensor("wlbuf1", [P, 8 * QW], F32, side="right"))
    actcol = oT
    x1q0 = _sc8.enter_context(nc.sbuf_tensor("x1q0", [P, QW], F32, side="right"))
    x1q1 = _sc8.enter_context(nc.sbuf_tensor("x1q1", [P, QW], F32, side="right"))
    outq0 = _sc8.enter_context(nc.sbuf_tensor("outq0", [P, QW], F32, side="right"))
    outq1 = _sc8.enter_context(nc.sbuf_tensor("outq1", [P, QW], F32, side="right"))
    gbB = _sc8.enter_context(nc.sbuf_tensor("gbB", [P, D], BF16, side="right"))
    ps_y0 = _sc8.enter_context(nc.psum_tensor("ps_y0", [P, 512], F32))
    ps_y1 = _sc8.enter_context(nc.psum_tensor("ps_y1", [P, 512], F32))
    ps_y2 = _sc8.enter_context(nc.psum_tensor("ps_y2", [P, 512], F32))
    ps_y3 = _sc8.enter_context(nc.psum_tensor("ps_y3", [P, 512], F32))
    ps_y4 = _sc8.enter_context(nc.psum_tensor("ps_y4", [P, 512], F32))
    ps_y5 = _sc8.enter_context(nc.psum_tensor("ps_y5", [P, 512], F32))
    ps_y6 = _sc8.enter_context(nc.psum_tensor("ps_y6", [P, 512], F32))
    ps_y7 = _sc8.enter_context(nc.psum_tensor("ps_y7", [P, 512], F32))
    if True:

        ps_y_sets = ((ps_y0, ps_y1, ps_y2, ps_y3), (ps_y4, ps_y5, ps_y6, ps_y7))
        gev2 = 0
        for i in range(4):
            td = pr.op("tensor", lambda e, i=i: e.matmul(
                ps_y_sets[1][i][:, :], ones_b[0:1, :],
                gbrow[0:1, i * 512:(i + 1) * 512],
                start=True, stop=True), inc=sT)
            gev2 = pr.op("vector", lambda e, i=i: e.tensor_copy(
                out=gbB[:, i * 512:(i + 1) * 512], in_=ps_y_sets[1][i][:, :]),
                waits=[(sT, td)], inc=sV)
        gbB_done = gev2

        wlb = (wlb0, wlb1)
        wlsem = (dA, dB)
        x1qsem = (dF, dG)
        outsem = (dOut, dOut2)
        evq = [0, 0]
        vq_done = [[0, 0, 0, 0], [gbB_done] * 4]
        x1cnt = [0, 0]
        outw = [0, 0]
        gts = 0
        NCK = 4
        SPC = NS // NCK
        chunk_last_mm = [0] * NCK
        wl_group_last_mm = [0] * 8
        wl_cast = [0] * 8
        acnt_hist = 0
        for q in range(NQ):
            gcnt = [0, 0]
            ccast = [0, 0]
            for g in range(8):
                par = g % 2
                w = [(sV, ccast[par])] if g >= 2 else []
                gcnt[par] = dma("sync",
                                wlb[par][:, :].rearrange("p (s c) -> p s c", s=8),
                                wl_d[q][g * 8:(g + 1) * 8].rearrange("s p c -> p s c"),
                                wlsem[par], waits=w)
                ccast[par] = pr.op("vector", lambda e, g=g, par=par: e.tensor_copy(
                    out=wlbf[:, g * 8 * QW:(g + 1) * 8 * QW], in_=wlb[par][:, :]),
                    waits=[(wlsem[par], gcnt[par]), (sT, wl_group_last_mm[g])],
                    inc=sV)
                wl_cast[g] = ccast[par]
            for ts in range(TSUP):
                sp = gts % 2
                ps_y = ps_y_sets[sp]
                for tt_ in range(4):
                    t = ts * 4 + tt_
                    tp = tt_ % 2
                    x1cnt[tp] = dma("gpsimd",
                                    (x1q0 if tp == 0 else x1q1)[:, :],
                                    x1_hbm[t][:, q * QW:(q + 1) * QW],
                                    x1qsem[tp], waits=[(sV, evq[tp])])
                    if tt_ == 1:
                        break
                for ck in range(NCK):
                    w = [(sT, chunk_last_mm[ck])]
                    if q == 0 and ts == 0 and ck == 0:
                        w += [(dF, dF.n), (dG, dG.n), (dH, dH.n), (dI, dI.n)]
                    if acnt_hist:
                        w += [(dC, acnt_hist)]
                    acnt_hist = dma(
                        "sync",
                        actcol[:, ck * SPC * QW:(ck + 1) * SPC * QW].rearrange(
                            "p (s c) -> p s c", s=SPC),
                        act_hbm[ts][ck * SPC:(ck + 1) * SPC].rearrange(
                            "s p c -> p s c"),
                        dC, waits=w)
                    if ck == 0:
                        acnt0 = acnt_hist
                    chunk_acnt = acnt_hist
                    if ck == 0:
                        chunk_acnts = [0] * NCK
                    chunk_acnts[ck] = chunk_acnt
                mmd = 0
                for s in range(NS):
                    ck = s // SPC
                    for tt_ in range(4):
                        w = []
                        if s % SPC == 0 and tt_ == 0:
                            w += [(dC, chunk_acnts[ck])]
                        if s % 8 == 0 and tt_ == 0 and ts == 0:
                            w += [(sV, wl_cast[s // 8])]
                        if s == 0:
                            w += [(sV, vq_done[sp][tt_])]
                        do_inc = (tt_ == 3 and (s % 8 == 7))
                        mmd = pr.op("tensor", lambda e, s=s, tt_=tt_, py=ps_y[tt_]: e.matmul(
                            py[:, :],
                            actcol[:, s * QW + tt_ * P: s * QW + (tt_ + 1) * P],
                            wlbf[:, s * QW:(s + 1) * QW],
                            start=(s == 0), stop=(s == NS - 1)),
                            waits=w,
                            inc=(sT if do_inc else None))
                    if s % 8 == 7:
                        wl_group_last_mm[s // 8] = mmd
                    if s % SPC == SPC - 1:
                        chunk_last_mm[s // SPC] = mmd
                for tt_ in range(4):
                    t = ts * 4 + tt_
                    tp = tt_ % 2
                    x1q = x1q0 if tp == 0 else x1q1
                    outq = outq0 if tp == 0 else outq1
                    vq = pr.op("vector", lambda e, q=q, outq=outq, py=ps_y[tt_]:
                               e.tensor_tensor(
                                   out=outq[:, :], in0=py[:, :],
                                   in1=gbB[:, q * QW:(q + 1) * QW], op=ALU.mult),
                               waits=[(sT, mmd), (sV, gbB_done),
                                      (x1qsem[tp], x1cnt[tp]),
                                      (outsem[tp], outw[tp])], inc=sV)
                    vq_done[sp][tt_] = vq
                    evq[tp] = pr.op("vector", lambda e, outq=outq, x1q=x1q:
                                    e.tensor_tensor(out=outq[:, :], in0=outq[:, :],
                                                    in1=x1q[:, :], op=ALU.add),
                                    waits=[(sV, vq)], inc=sV)
                    outw[tp] = dma("gpsimd", y_d[t][:, q * QW:(q + 1) * QW],
                                   outq[:, :], outsem[tp], waits=[(sV, evq[tp])])
                    if tt_ < 2 and (ts + 1 < TSUP or q + 1 < NQ):
                        pass
                    if tt_ == 1:
                        for tt2 in (2, 3):
                            t2 = ts * 4 + tt2
                            tp2 = tt2 % 2
                            x1cnt[tp2] = dma("gpsimd",
                                             (x1q0 if tp2 == 0 else x1q1)[:, :],
                                             x1_hbm[t2][:, q * QW:(q + 1) * QW],
                                             x1qsem[tp2], waits=[(sV, evq[tp2])])
                gts += 1
        pr.op("gpsimd", lambda e: e.engine_nop(),
              waits=[(dOut, dOut.n), (dOut2, dOut2.n)])
        pr.flush()

    _sc8.close()
    es.close()
    return nc, dbg


_BUILT = None


def _host_tiles(inputs):
    f32 = np.float32
    Wq = np.asarray(inputs["L0_attn_Wq"], f32)
    Wk = np.asarray(inputs["L0_attn_Wk"], f32)
    Wv = np.asarray(inputs["L0_attn_Wv"], f32)
    wqkv = np.ascontiguousarray(np.stack([
        Wq.reshape(DT, P, H, E).transpose(2, 0, 1, 3),
        Wk.reshape(DT, P, H, E).transpose(2, 0, 1, 3),
        Wv.reshape(DT, P, H, E).transpose(2, 0, 1, 3),
    ], axis=1))
    wmod = np.ascontiguousarray(
        np.asarray(inputs["L0_wmod"], f32).reshape(DT, P, NMS, MSW).transpose(2, 0, 1, 3))
    ccol = np.ascontiguousarray(
        np.asarray(inputs["L0_c"], f32).reshape(8, DT, P).transpose(0, 2, 1))
    wf = np.ascontiguousarray(np.asarray(inputs["L0_attn_Wf"], f32).reshape(DT, P, H))
    wo = np.ascontiguousarray(np.asarray(inputs["L0_attn_Wo"], f32).reshape(DT, P, D))
    nwcol_ = np.ascontiguousarray(
        np.asarray(inputs["L0_attn_norm_w"], f32).reshape(DT, P).T)
    wg_in = np.asarray(inputs["L0_ffn_w_gating"], f32)
    wg = np.ascontiguousarray(
        np.stack([wg_in[0].reshape(DT, P, NS, E).transpose(2, 0, 1, 3),
                  wg_in[1].reshape(DT, P, NS, E).transpose(2, 0, 1, 3)],
                 axis=1))
    wl = np.ascontiguousarray(
        np.asarray(inputs["L0_ffn_w_linear"], f32).reshape(NS, P, NQ, QW).transpose(2, 0, 1, 3))
    return wqkv, wmod, ccol, wf, wo, nwcol_, wg, wl


def kernel(**inputs):
    global _BUILT
    if _BUILT is None:
        _BUILT = build(debug=False)
    nc, _ = _BUILT
    wqkv, wmod, ccol, wf, wo, nwcol_, wg, wl = _host_tiles(inputs)
    x = np.ascontiguousarray(np.asarray(inputs["L0_attn_x"], np.float32))
    in_maps = []
    for b in range(8):
        in_maps.append({
            "x": x[b].reshape(TT, P, D),
            "wmod": wmod, "ccol": np.ascontiguousarray(ccol[b]),
            "wqkv": wqkv, "wf": wf, "wo": wo, "nwcol": nwcol_,
            "wg": wg, "wl": wl,
        })
    res = run_bass_kernel_spmd(nc, in_maps, core_ids=list(range(8)))
    out = np.stack([r["y"].reshape(D, D) for r in res.results])
    return out


# version 07
# speedup vs baseline: 1.0088x; 1.0088x over previous
import numpy as np
from contextlib import ExitStack

import concourse.bass as bass
import concourse.mybir as mybir
from concourse.bass_utils import run_bass_kernel_spmd

F32 = mybir.dt.float32
F32R = mybir.dt.float32r
BF16 = mybir.dt.bfloat16
AF = mybir.ActivationFunctionType
ALU = mybir.AluOpType
AX = mybir.AxisListType

P = 128
D = 2048
TT = 16
DT = 16
H = 16
E = 128
NCH = 16
MS = 12288
NMS = 48
MSW = 256
HID = 8192
NS = 64
NQ = 4
QW = 512
TSUP = 4
LNEPS = 1e-5
RMSEPS = 1e-6


class Sem:
    def __init__(self, nc, es, name):
        self.h = es.enter_context(nc.semaphore(name))
        self.n = 0
        self.name = name


class Prog:
    ENGS = ("tensor", "vector", "scalar", "sync", "gpsimd")

    def __init__(self, nc):
        self.nc = nc
        self.q = {e: [] for e in self.ENGS}
        self.last_wait = {}

    def _wait(self, eng, sem, val):
        if val <= 0:
            return
        key = (eng, sem.name)
        if self.last_wait.get(key, 0) >= val:
            return
        self.last_wait[key] = val
        self.q[eng].append(lambda e, h=sem.h, v=val: e.wait_ge(h, v))

    def op(self, eng, fn, waits=(), inc=None, inc_k=1):
        for sem, val in waits:
            self._wait(eng, sem, val)
        if inc is not None:
            self.q[eng].append(lambda e, fn=fn, h=inc.h, k=inc_k: fn(e).then_inc(h, k))
            inc.n += inc_k
            return inc.n
        self.q[eng].append(fn)
        return None

    def barrier(self, sems):
        for eng in self.ENGS:
            for s in sems:
                self._wait(eng, s, s.n)

    def flush(self):
        with self.nc.Block() as block:
            for eng in self.ENGS:
                ops = self.q[eng]
                if ops:
                    getattr(block, eng)(lambda e, ops=ops: [f(e) for f in ops])
        self.q = {e: [] for e in self.ENGS}


def build(debug=False):
    nc = bass.Bass("TRN2", target_bir_lowering=False)
    import ml_dtypes

    x_d = nc.dram_tensor("x", [TT, P, D], F32, kind="ExternalInput").ap()
    wmod_d = nc.dram_tensor("wmod", [NMS, DT, P, MSW], F32, kind="ExternalInput").ap()
    ccol_d = nc.dram_tensor("ccol", [P, DT], F32, kind="ExternalInput").ap()
    wqkv_d = nc.dram_tensor("wqkv", [H, 3, DT, P, E], F32, kind="ExternalInput").ap()
    wf_d = nc.dram_tensor("wf", [DT, P, H], F32, kind="ExternalInput").ap()
    wo_d = nc.dram_tensor("wo", [DT, P, D], F32, kind="ExternalInput").ap()
    nw_d = nc.dram_tensor("nwcol", [P, DT], F32, kind="ExternalInput").ap()
    wg_d = nc.dram_tensor("wg", [NS, 2, DT, P, E], F32, kind="ExternalInput").ap()
    wl_d = nc.dram_tensor("wl", [NQ, NS, P, QW], F32, kind="ExternalInput").ap()
    y_d = nc.dram_tensor("y", [TT, P, D], F32, kind="ExternalOutput").ap()

    x1_hbm = nc.dram_tensor("x1_scratch", [TT, P, D], F32, kind="Internal").ap()
    act_hbm = nc.dram_tensor("act_scratch", [TSUP, NS, P, QW], BF16, kind="Internal").ap()

    eye_bf_d = nc.inline_tensor(np.eye(P, dtype=ml_dtypes.bfloat16), "eye_bf_c")
    eye16_d = nc.inline_tensor(np.eye(16, dtype=np.float32), "eye16_c")
    tri_np = np.triu(np.ones((P, P), dtype=np.float32))
    tri_d = nc.inline_tensor(tri_np, "tri_c")
    mask_d = nc.inline_tensor(tri_np.astype(ml_dtypes.bfloat16), "mask_c")
    ones_f_d = nc.inline_tensor(np.ones((1, P), dtype=np.float32), "ones_f_c")
    ones_b_d = nc.inline_tensor(np.ones((1, P), dtype=ml_dtypes.bfloat16), "ones_b_c")
    onescol_d = nc.inline_tensor(np.ones((P, 1), dtype=np.float32), "ones_col_c")
    constvals = [LNEPS, RMSEPS]
    constv_d = nc.inline_tensor(
        np.broadcast_to(np.asarray(constvals, np.float32), (P, len(constvals))).copy(),
        "constv_c")

    dbg = {}
    es = ExitStack()

    def sbl(name, shape, dt):
        return es.enter_context(nc.sbuf_tensor(name, shape, dt, side="left"))

    eye_bf = sbl("eye_bf", [P, P], BF16)
    eye16 = sbl("eye16", [16, 16], F32)
    tri = sbl("tri", [P, P], F32)
    maskb = sbl("maskb", [P, P], BF16)
    ones_f = sbl("ones_f", [1, P], F32)
    ones_b = sbl("ones_b", [1, P], BF16)
    sacol = sbl("sacol", [P, DT], F32)
    cacol = sbl("cacol", [P, DT], F32)
    sbcol = sbl("sbcol", [P, DT], F32)
    cbcol = sbl("cbcol", [P, DT], F32)
    garow = sbl("garow", [1, D], BF16)
    gbrow = sbl("gbrow", [1, D], BF16)
    mean1 = sbl("mean1", [P, TT], F32)
    rstd1 = sbl("rstd1", [P, TT], F32)
    mean2 = sbl("mean2", [P, TT], F32)
    rstd2 = sbl("rstd2", [P, TT], F32)
    msum = sbl("msum", [P, TT], F32)
    sqsum = sbl("sqsum", [P, TT], F32)
    ssq = sbl("ssq", [P, TT], F32)
    ssqt = sbl("ssqt", [P, 1], F32)
    rmsr = sbl("rmsr", [P, TT], F32)
    cumall = sbl("cumall", [P, NCH * H], F32)
    e1t = sbl("e1t", [P, NCH * H], F32)
    e2t = sbl("e2t", [P, NCH * H], F32)
    e3t = sbl("e3t", [P, (NCH - 1) * H], F32)
    e4bc = sbl("e4bc", [P, (NCH - 1) * H], F32)
    ldtok = sbl("ldtok", [P, NCH * H], F32)
    Sbf = sbl("Sbf", [P, E], BF16)
    tmp1 = sbl("tmp1", [P, NCH * H], F32)
    nwcol = sbl("nwcol_sb", [P, DT], F32)
    oT = sbl("oT", [P, H * D], BF16)
    constv = sbl("constv", [P, 2], F32)
    onescol = sbl("onescol", [P, 1], F32)
    totrow = sbl("totrow", [1, NCH * H], F32)
    nc.const_aps.register(F32, LNEPS, constv[:, 0:1])
    nc.const_aps.register(F32, RMSEPS, constv[:, 1:2])

    sT = Sem(nc, es, "sT")
    sV = Sem(nc, es, "sV")
    sS = Sem(nc, es, "sS")
    dA = Sem(nc, es, "dA")
    dB = Sem(nc, es, "dB")
    dC = Sem(nc, es, "dC")
    dE = Sem(nc, es, "dE")
    dF = Sem(nc, es, "dF")
    dG = Sem(nc, es, "dG")
    dOut = Sem(nc, es, "dOut")
    dOut2 = Sem(nc, es, "dOut2")
    dH = Sem(nc, es, "dH")
    dI = Sem(nc, es, "dI")
    ALLSEMS = [sT, sV, sS, dA, dB, dC, dE, dF, dG, dOut, dOut2, dH, dI]

    pr = Prog(nc)

    def dma(qeng, dst, src, sem, waits=()):
        return pr.op(qeng, lambda e, d=dst, s=src: e.dma_start(out=d, in_=s),
                     waits=waits, inc=sem, inc_k=16)

    def BARRIER():
        pr.barrier(ALLSEMS)

    _sc1 = ExitStack()
    wmA = _sc1.enter_context(nc.sbuf_tensor("wmbufA", [P, DT * MSW], BF16, side="right"))
    wmB = _sc1.enter_context(nc.sbuf_tensor("wmbufB", [P, DT * MSW], BF16, side="right"))
    modrow = _sc1.enter_context(nc.sbuf_tensor("modrow", [1, MS], F32, side="right"))
    ccol = _sc1.enter_context(nc.sbuf_tensor("ccol_sb", [P, DT], F32, side="right"))
    silc = _sc1.enter_context(nc.sbuf_tensor("silc", [P, DT], F32, side="right"))
    silcb = _sc1.enter_context(nc.sbuf_tensor("silcb", [P, DT], BF16, side="right"))
    grow = _sc1.enter_context(nc.sbuf_tensor("grow", [1, D], F32, side="right"))
    grow2 = _sc1.enter_context(nc.sbuf_tensor("grow2", [1, D], F32, side="right"))
    xrA = _sc1.enter_context(nc.sbuf_tensor("xrA", [P, D], F32, side="right"))
    xrB = _sc1.enter_context(nc.sbuf_tensor("xrB", [P, D], F32, side="right"))
    junk1 = _sc1.enter_context(nc.sbuf_tensor("junk1", [P, D], BF16, side="right"))
    ps_m0 = _sc1.enter_context(nc.psum_tensor("ps_m0", [P, 512], F32))
    ps_m1 = _sc1.enter_context(nc.psum_tensor("ps_m1", [P, 512], F32))
    ps_col = _sc1.enter_context(nc.psum_tensor("ps_col", [P, 512], F32))
    if True:

        for dst, src in ((eye_bf, eye_bf_d), (eye16, eye16_d), (tri, tri_d),
                         (maskb, mask_d), (ones_f, ones_f_d), (ones_b, ones_b_d)):
            dma("sync", dst[:, :], src.ap(), dC)
        dma("sync", constv[:, :], constv_d.ap(), dC)
        dma("sync", onescol[:, :], onescol_d.ap(), dC)
        dma("sync", ccol[:, :], ccol_d, dC)
        dma("sync", nwcol[:, :], nw_d, dC)
        cdone = dC.n

        sil_done = pr.op("scalar", lambda e: e.activation(
            out=silc[:, :], in_=ccol[:, :], func=AF.Silu),
            waits=[(dC, cdone)], inc=sS)
        silb_done = pr.op("vector", lambda e: e.tensor_copy(
            out=silcb[:, :], in_=silc[:, :]), waits=[(sS, sil_done)], inc=sV)

        xrP = (xrA, xrB)
        xsemP = (dC, dE)
        vdoneP = [0, 0]
        sdoneP = [0, 0]
        for t in range(TT):
            par = t % 2
            w = ([(sV, vdoneP[par]), (sS, sdoneP[par])] if t >= 2
                 else [(dC, cdone)])
            cnt = dma("sync", xrP[par][:, :], x_d[t], xsemP[par], waits=w)
            vdoneP[par] = pr.op("vector", lambda e, par=par, t=t: e.tensor_reduce(
                out=msum[:, t:t + 1], in_=xrP[par][:, :], axis=AX.X, op=ALU.add),
                waits=[(xsemP[par], cnt)], inc=sV)
            sdoneP[par] = pr.op("scalar", lambda e, par=par, t=t: e.activation(
                out=junk1[:, :], in_=xrP[par][:, :], func=AF.Square,
                accum_out=sqsum[:, t:t + 1]),
                waits=[(xsemP[par], cnt), (sS, max(sdoneP))], inc=sS)

        mv_done = []
        ev_par = [0, 0]
        for ms in range(NMS):
            par = ms % 2
            buf = wmA if par == 0 else wmB
            dsem = dF if par == 0 else dG
            w = [(sT, mv_done[-2])] if len(mv_done) >= 2 else []
            cnt = dma("gpsimd", buf[:, :].rearrange("p (dt c) -> p dt c", dt=DT),
                      wmod_d[ms].rearrange("dt p c -> p dt c"), dsem, waits=w)
            psm = ps_m0 if par == 0 else ps_m1
            md = None
            for dt_ in range(DT):
                md = pr.op("tensor",
                      lambda e, dt_=dt_, buf=buf, psm=psm: e.matmul(
                          psm[0:1, 0:MSW], silcb[:, dt_:dt_ + 1],
                          buf[:, dt_ * MSW:(dt_ + 1) * MSW],
                          start=(dt_ == 0), stop=(dt_ == DT - 1)),
                      waits=([(dsem, cnt), (sV, silb_done), (sV, ev_par[par])]
                             if dt_ == 0 else []),
                      inc=(sT if dt_ == DT - 1 else None))
            mv_done.append(md)
            ev_par[par] = pr.op("vector", lambda e, ms=ms, psm=psm: e.tensor_copy(
                out=modrow[0:1, ms * MSW:(ms + 1) * MSW], in_=psm[0:1, 0:MSW]),
                waits=[(sT, md)], inc=sV)
        mod_done = sV.n

        col_specs = [(0, sacol, 0.0), (1, cacol, 1.0), (3, sbcol, 0.0), (4, cbcol, 1.0)]
        cols_mm = None
        for vi, (seg, dstcol, addone) in enumerate(col_specs):
            for t in range(DT):
                cols_mm = pr.op("tensor", lambda e, seg=seg, t=t, vi=vi: e.matmul(
                    ps_col[:, vi * 16 + t:vi * 16 + t + 1],
                    modrow[0:1, seg * D + t * P: seg * D + (t + 1) * P],
                    ones_f[0:1, 0:1], start=True, stop=True),
                    waits=[(sV, mod_done)] if (vi == 0 and t == 0) else [],
                    inc=(sT if (vi == 3 and t == DT - 1) else None))
        for vi, (seg, dstcol, addone) in enumerate(col_specs):
            pr.op("vector", lambda e, vi=vi, dstcol=dstcol, addone=addone: e.tensor_scalar(
                out=dstcol[:, :], in0=ps_col[:, vi * 16:(vi + 1) * 16],
                scalar1=float(addone), scalar2=None, op0=ALU.add),
                waits=[(sT, cols_mm)], inc=sV)

        gprev = 0
        gprev_s = 0
        for seg, dst in ((2, garow), (5, gbrow)):
            src = modrow[0:1, seg * D:(seg + 1) * D]
            g1 = pr.op("scalar", lambda e, src=src: e.activation(
                out=grow[0:1, :], in_=src, func=AF.Sign),
                waits=[(sV, mod_done), (sV, gprev), (sS, gprev_s)], inc=sS)
            g2 = pr.op("scalar", lambda e, src=src: e.activation(
                out=grow2[0:1, :], in_=src, func=AF.Abs),
                waits=[(sS, g1)], inc=sS)
            gprev_s = g2
            g3 = pr.op("vector", lambda e: e.tensor_scalar_max(
                out=grow2[0:1, :], in0=grow2[0:1, :], scalar1=0.25),
                waits=[(sS, g2)], inc=sV)
            gprev = pr.op("vector", lambda e, dst=dst: e.tensor_tensor(
                out=dst[0:1, :], in0=grow[0:1, :], in1=grow2[0:1, :], op=ALU.mult),
                waits=[(sV, g3)], inc=sV)
        if debug:
            dbg["modrow"] = nc.dram_tensor("dbg_modrow", [1, MS], F32,
                                           kind="ExternalOutput").ap()
            dma("gpsimd", dbg["modrow"], modrow[0:1, :], dOut, waits=[(sV, sV.n)])
        BARRIER()
        pr.flush()
    _sc1.close()

    h1T = es.enter_context(nc.sbuf_tensor("h1T", [P, DT * D], BF16, side="left"))

    _sc2 = ExitStack()
    xr0 = _sc2.enter_context(nc.sbuf_tensor("xr0", [P, D], F32, side="right"))
    xr1 = _sc2.enter_context(nc.sbuf_tensor("xr1", [P, D], F32, side="right"))
    junk2 = _sc2.enter_context(nc.sbuf_tensor("junk2", [P, D], BF16, side="right"))
    lnxb0 = _sc2.enter_context(nc.sbuf_tensor("lnxb0", [P, D], BF16, side="right"))
    lnxb1 = _sc2.enter_context(nc.sbuf_tensor("lnxb1", [P, D], BF16, side="right"))
    ps_t0 = _sc2.enter_context(nc.psum_tensor("ps_t0", [P, 1024], BF16))
    ps_t1 = _sc2.enter_context(nc.psum_tensor("ps_t1", [P, 1024], BF16))
    if True:

        xr = (xr0, xr1)
        xsem = (dA, dB)
        vdone = [0, 0]
        mean_done = pr.op("vector", lambda e: e.tensor_scalar(
            out=mean1[:, :], in0=msum[:, :], scalar1=1.0 / D, scalar2=None,
            op0=ALU.mult), waits=[(sV, sV.n), (sS, sS.n)], inc=sV)
        c1 = pr.op("vector", lambda e: e.tensor_scalar(
            out=tmp1[:, 0:TT], in0=sqsum[:, :], scalar1=1.0 / D, scalar2=None,
            op0=ALU.mult), waits=[(sV, mean_done)], inc=sV)
        c2 = pr.op("vector", lambda e: e.tensor_tensor(
            out=tmp1[:, 16:32], in0=mean1[:, :], in1=mean1[:, :], op=ALU.mult),
            waits=[(sV, c1)], inc=sV)
        vstat = pr.op("vector", lambda e: e.tensor_tensor(
            out=tmp1[:, 0:TT], in0=tmp1[:, 0:TT], in1=tmp1[:, 16:32],
            op=ALU.subtract), waits=[(sV, c2)], inc=sV)
        ln1d = pr.op("scalar", lambda e: e.activation(
            out=tmp1[:, 0:TT], in_=tmp1[:, 0:TT], func=AF.Ln, bias=LNEPS, scale=1.0),
            waits=[(sV, vstat)], inc=sS)
        rstd_done = pr.op("scalar", lambda e: e.activation(
            out=rstd1[:, :], in_=tmp1[:, 0:TT], func=AF.Exp, scale=-0.5),
            waits=[(sS, ln1d)], inc=sS)

        tile_tr_done = [0, 0]
        ev_done = [0, 0]
        for t in range(TT):
            par = t % 2
            lnxb = lnxb0 if par == 0 else lnxb1
            w = [(sV, vdone[par]), (sS, rstd_done)]
            cnt = dma("sync", xr[par][:, :], x_d[t], xsem[par], waits=w)
            vdone[par] = pr.op("vector", lambda e, par=par, t=t, lnxb=lnxb: e.tensor_scalar(
                out=lnxb[:, :], in0=xr[par][:, :],
                scalar1=mean1[:, t:t + 1], scalar2=rstd1[:, t:t + 1],
                op0=ALU.subtract, op1=ALU.mult),
                waits=[(xsem[par], cnt), (sS, rstd_done), (sV, mean_done),
                       (sT, tile_tr_done[par])], inc=sV)
            for dt_ in range(DT):
                ppar = dt_ % 2
                ptr = ps_t0 if ppar == 0 else ps_t1
                td = pr.op("tensor", lambda e, dt_=dt_, ptr=ptr, lnxb=lnxb: e.transpose(
                    ptr[:, 0:P], lnxb[:, dt_ * P:(dt_ + 1) * P], eye_bf[:, :]),
                    waits=[(sV, vdone[par]), (sS, ev_done[ppar])], inc=sT)
                ev_done[ppar] = pr.op("scalar", lambda e, dt_=dt_, t=t, ptr=ptr:
                                      e.activation(
                                          out=h1T[:, dt_ * D + t * P: dt_ * D + (t + 1) * P],
                                          in_=ptr[:, 0:P], func=AF.Identity,
                                          scale=cacol[:, dt_:dt_ + 1],
                                          bias=sacol[:, dt_:dt_ + 1]),
                                      waits=[(sT, td)], inc=sS)
            tile_tr_done[par] = sT.n
        BARRIER()
        if debug:
            dbg["h1T"] = nc.dram_tensor("dbg_h1T", [P, DT * D], BF16,
                                        kind="ExternalOutput").ap()
            dma("gpsimd", dbg["h1T"], h1T[:, :], dOut)
            BARRIER()
        pr.flush()
    _sc2.close()

    _sc3 = ExitStack()
    wfbuf = _sc3.enter_context(nc.sbuf_tensor("wfbuf", [P, DT * H], F32, side="right"))
    wfbf = _sc3.enter_context(nc.sbuf_tensor("wfbf", [P, DT * H], BF16, side="right"))
    ldT = _sc3.enter_context(nc.sbuf_tensor("ldT", [16, D], F32, side="right"))
    totbc = _sc3.enter_context(nc.sbuf_tensor("totbc", [P, NCH * H], F32, side="right"))
    ps_l0 = _sc3.enter_context(nc.psum_tensor("ps_l0", [16, 512], F32))
    ps_l1 = _sc3.enter_context(nc.psum_tensor("ps_l1", [16, 512], F32))
    ps_c0 = _sc3.enter_context(nc.psum_tensor("ps_c0", [P, 512], F32))
    ps_c1 = _sc3.enter_context(nc.psum_tensor("ps_c1", [P, 512], F32))
    if True:

        ps_c = (ps_c0, ps_c1)
        cnt = dma("sync", wfbuf[:, :].rearrange("p (dt h) -> p dt h", dt=DT), wf_d.rearrange("dt p h -> p dt h"), dA)
        wfc = pr.op("vector", lambda e: e.tensor_copy(out=wfbf[:, :], in_=wfbuf[:, :]),
                    waits=[(dA, cnt)], inc=sV)
        ld_ev = [0, 0]
        for ch in range(4):
            lp = ch % 2
            pld = ps_l0 if lp == 0 else ps_l1
            md = None
            for dt_ in range(DT):
                md = pr.op("tensor", lambda e, ch=ch, dt_=dt_, pld=pld: e.matmul(
                    pld[:, :], wfbf[:, dt_ * H:(dt_ + 1) * H],
                    h1T[:, dt_ * D + ch * 512: dt_ * D + (ch + 1) * 512],
                    start=(dt_ == 0), stop=(dt_ == DT - 1)),
                    waits=[(sV, wfc), (sS, ld_ev[lp])] if dt_ == 0 else [],
                    inc=(sT if dt_ == DT - 1 else None))
            ld_ev[lp] = pr.op("scalar", lambda e, ch=ch, pld=pld: e.activation(
                out=ldT[:, ch * 512:(ch + 1) * 512], in_=pld[:, :], func=AF.Sigmoid),
                waits=[(sT, md)], inc=sS)
        lds = pr.op("scalar", lambda e: e.activation(
            out=ldT[:, :], in_=ldT[:, :], func=AF.Ln),
            waits=[(sS, max(ld_ev))], inc=sS)

        pev = [0, 0]
        for c in range(NCH):
            cp = c % 2
            td = pr.op("tensor", lambda e, c=c, cp=cp: e.transpose(
                ps_c[cp][0:P, 0:16], ldT[:, c * P:(c + 1) * P], eye16[:, :]),
                waits=[(sS, lds), (sV, pev[cp])], inc=sT)
            pev[cp] = pr.op("vector", lambda e, c=c, cp=cp: e.tensor_copy(
                out=ldtok[:, c * H:(c + 1) * H], in_=ps_c[cp][0:P, 0:16]),
                waits=[(sT, td)], inc=sV)
        ld_all = sV.n
        for c in range(NCH):
            cp = c % 2
            pr.op("tensor", lambda e, c=c, cp=cp: e.matmul(
                ps_c[cp][:, 0:16], tri[:, :], ldtok[:, c * H:(c + 1) * H],
                start=True, stop=True),
                waits=[(sV, ld_all), (sV, pev[cp])])
            td = pr.op("tensor", lambda e, c=c, cp=cp: e.matmul(
                ps_c[cp][0:1, 16:32], onescol[:, :], ldtok[:, c * H:(c + 1) * H],
                start=True, stop=True), inc=sT)
            pr.op("vector", lambda e, c=c, cp=cp: e.tensor_copy(
                out=cumall[:, c * H:(c + 1) * H], in_=ps_c[cp][:, 0:16]),
                waits=[(sT, td)])
            pev[cp] = pr.op("vector", lambda e, c=c, cp=cp: e.tensor_copy(
                out=totrow[0:1, c * H:(c + 1) * H], in_=ps_c[cp][0:1, 16:32]),
                inc=sV)
        cum_done = sV.n

        nbc = NCH * H // P
        for i in range(nbc):
            cp = i % 2
            td = pr.op("tensor", lambda e, i=i, cp=cp: e.matmul(
                ps_c[cp][:, 0:P], ones_f[0:1, :], totrow[0:1, i * P:(i + 1) * P],
                start=True, stop=True), waits=[(sV, cum_done), (sV, pev[cp])], inc=sT)
            pev[cp] = pr.op("vector", lambda e, i=i, cp=cp: e.tensor_copy(
                out=totbc[:, i * P:(i + 1) * P], in_=ps_c[cp][:, 0:P]),
                waits=[(sT, td)], inc=sV)
        bc_ev = sV.n

        vt = pr.op("vector", lambda e: e.scalar_tensor_tensor(
            out=tmp1[:, :], in0=totbc[:, :], scalar=0.5, in1=cumall[:, :],
            op0=ALU.mult, op1=ALU.subtract), waits=[(sV, bc_ev)], inc=sV)
        pr.op("scalar", lambda e: e.activation(out=e1t[:, :], in_=tmp1[:, :],
                                               func=AF.Exp), waits=[(sV, vt)])
        e12 = pr.op("scalar", lambda e: e.activation(out=e2t[:, :], in_=tmp1[:, :],
                                                     func=AF.Exp, scale=-1.0), inc=sS)
        NE = (NCH - 1) * H
        p3 = pr.op("vector", lambda e: e.tensor_tensor(
            out=tmp1[:, 0:NE], in0=totbc[:, 0:NE], in1=cumall[:, 0:NE],
            op=ALU.subtract), waits=[(sS, e12)], inc=sV)
        vt3 = pr.op("vector", lambda e: e.scalar_tensor_tensor(
            out=tmp1[:, 0:NE], in0=totbc[:, H:H + NE], scalar=0.5, in1=tmp1[:, 0:NE],
            op0=ALU.mult, op1=ALU.add), waits=[(sV, p3)], inc=sV)
        e3d = pr.op("scalar", lambda e: e.activation(
            out=e3t[:, :], in_=tmp1[:, 0:NE], func=AF.Exp), waits=[(sV, vt3)], inc=sS)
        vt4 = pr.op("vector", lambda e: e.tensor_tensor(
            out=tmp1[0:1, 0:NE], in0=totrow[0:1, 0:NE],
            in1=totrow[0:1, H:H + NE], op=ALU.add), waits=[(sS, e3d)], inc=sV)
        e4r = pr.op("scalar", lambda e: e.activation(
            out=tmp1[0:1, 0:NE], in_=tmp1[0:1, 0:NE], func=AF.Exp, scale=0.5),
            waits=[(sV, vt4)], inc=sS)
        for i in range((NE + P - 1) // P):
            w_ = min(P, NE - i * P)
            cp = i % 2
            td = pr.op("tensor", lambda e, i=i, w_=w_, cp=cp: e.matmul(
                ps_c[cp][:, 0:w_], ones_f[0:1, :], tmp1[0:1, i * P:i * P + w_],
                start=True, stop=True), waits=[(sS, e4r), (sV, pev[cp])], inc=sT)
            pev[cp] = pr.op("vector", lambda e, i=i, w_=w_, cp=cp: e.tensor_copy(
                out=e4bc[:, i * P:i * P + w_], in_=ps_c[cp][:, 0:w_]),
                waits=[(sT, td)], inc=sV)
        pr.op("vector", lambda e: e.memset(ssq[:, :], 0.0), inc=sV)
        BARRIER()
        if debug:
            for nm, tens, wdt in (("ldtok", ldtok, NCH * H), ("cumall", cumall, NCH * H),
                                  ("e1t", e1t, NCH * H), ("e2t", e2t, NCH * H),
                                  ("e3t", e3t, NE), ("e4bc", e4bc, NE)):
                dbg[nm] = nc.dram_tensor("dbg_" + nm, [P, wdt], F32,
                                         kind="ExternalOutput").ap()
                dma("gpsimd", dbg[nm], tens[:, :], dOut)
            BARRIER()
        pr.flush()
    _sc3.close()

    _sc4 = ExitStack()
    wbuf = _sc4.enter_context(nc.sbuf_tensor("wbuf", [P, 3 * D], F32, side="right"))
    wqbf = _sc4.enter_context(nc.sbuf_tensor("wqbf", [P, D], BF16, side="right"))
    wkbf = _sc4.enter_context(nc.sbuf_tensor("wkbf", [P, D], BF16, side="right"))
    wvbf = _sc4.enter_context(nc.sbuf_tensor("wvbf", [P, D], BF16, side="right"))
    qT = _sc4.enter_context(nc.sbuf_tensor("qT", [P, D], BF16, side="right"))
    kT = _sc4.enter_context(nc.sbuf_tensor("kT", [P, D], BF16, side="right"))
    vT = _sc4.enter_context(nc.sbuf_tensor("vT", [P, D], BF16, side="right"))
    ktok = _sc4.enter_context(nc.sbuf_tensor("ktok", [P, D], BF16, side="right"))
    vtok = _sc4.enter_context(nc.sbuf_tensor("vtok", [P, D], BF16, side="right"))
    s3b0 = _sc4.enter_context(nc.sbuf_tensor("s3b0", [P, P], BF16, side="right"))
    s3b1 = _sc4.enter_context(nc.sbuf_tensor("s3b1", [P, P], BF16, side="right"))
    ob0 = _sc4.enter_context(nc.sbuf_tensor("ob0", [P, P], BF16, side="right"))
    ob1 = _sc4.enter_context(nc.sbuf_tensor("ob1", [P, P], BF16, side="right"))
    kd0 = _sc4.enter_context(nc.sbuf_tensor("kd0", [P, P], BF16, side="right"))
    kd1 = _sc4.enter_context(nc.sbuf_tensor("kd1", [P, P], BF16, side="right"))
    junk5 = _sc4.enter_context(nc.sbuf_tensor("junk5", [P, P], BF16, side="right"))
    ps_p0 = _sc4.enter_context(nc.psum_tensor("ps_p0", [P, 512], F32))
    ps_p1 = _sc4.enter_context(nc.psum_tensor("ps_p1", [P, 512], F32))
    ps_s = _sc4.enter_context(nc.psum_tensor("ps_s", [P, 512], F32))
    ps_o0 = _sc4.enter_context(nc.psum_tensor("ps_o0", [P, 512], F32))
    ps_o1 = _sc4.enter_context(nc.psum_tensor("ps_o1", [P, 512], F32))
    ps_S = _sc4.enter_context(nc.psum_tensor("ps_S", [P, 512], F32))
    ps_x0 = _sc4.enter_context(nc.psum_tensor("ps_x0", [P, 1024], BF16))
    ps_x1 = _sc4.enter_context(nc.psum_tensor("ps_x1", [P, 1024], BF16))
    if True:

        wsem = (dA, dB)
        wcnt = [0, 0]
        proj_ev = [0, 0]
        tr_ev = [0, 0]
        last_m2m3 = 0
        mat_mm_done = [0, 0, 0]
        cast_done = 0
        cast_last = 0

        wcnt[0] = dma("sync", wbuf[:, :].rearrange("p (m dt e) -> p m dt e", m=3, dt=DT),
                      wqkv_d[0].rearrange("m dt p e -> p m dt e"), wsem[0])
        pr.op("vector", lambda e: e.tensor_copy(out=wqbf[:, :], in_=wbuf[:, 0:D]),
              waits=[(wsem[0], wcnt[0])])
        pr.op("vector", lambda e: e.tensor_copy(out=wkbf[:, :], in_=wbuf[:, D:2 * D]))
        cast_done = pr.op("vector", lambda e: e.tensor_copy(
            out=wvbf[:, :], in_=wbuf[:, 2 * D:3 * D]), inc=sV)

        for h in range(H):
            par = h % 2

            def proj(wbf, dstT, func, mat):
                lastmm = 0
                for ch in range(4):
                    ppar = ch % 2
                    pp = ps_p0 if ppar == 0 else ps_p1
                    md = None
                    for dt_ in range(DT):
                        md = pr.op("tensor", lambda e, ch=ch, dt_=dt_, pp=pp, wbf=wbf:
                              e.matmul(pp[:, :], wbf[:, dt_ * P:(dt_ + 1) * P],
                                       h1T[:, dt_ * D + ch * 512: dt_ * D + (ch + 1) * 512],
                                       start=(dt_ == 0), stop=(dt_ == DT - 1)),
                              waits=([(sV, cast_done), (sS, proj_ev[ppar])]
                                     if dt_ == 0 else []),
                              inc=(sT if dt_ == DT - 1 else None))
                    lastmm = md
                    proj_ev[ppar] = pr.op(
                        "scalar",
                        lambda e, ch=ch, pp=pp, dstT=dstT, func=func: e.activation(
                            out=dstT[:, ch * 512:(ch + 1) * 512], in_=pp[:, :],
                            func=func),
                        waits=[(sT, md)], inc=sS)
                mat_mm_done[mat] = lastmm
                return sS.n

            qd = proj(wqbf, qT, AF.Silu, 0)
            kdn = proj(wkbf, kT, AF.Silu, 1)
            vd = proj(wvbf, vT, AF.Copy, 2)

            if h + 1 < H:
                wcnt[1 - par] = dma("sync",
                                    wbuf[:, :].rearrange("p (m dt e) -> p m dt e",
                                                         m=3, dt=DT),
                                    wqkv_d[h + 1].rearrange("m dt p e -> p m dt e"),
                                    wsem[1 - par],
                                    waits=[(sV, cast_last if h > 0 else cast_done)])
                pr.op("vector", lambda e: e.tensor_copy(
                    out=wqbf[:, :], in_=wbuf[:, 0:D]),
                    waits=[(wsem[1 - par], wcnt[1 - par]), (sT, mat_mm_done[0])])
                pr.op("vector", lambda e: e.tensor_copy(
                    out=wkbf[:, :], in_=wbuf[:, D:2 * D]),
                    waits=[(sT, mat_mm_done[1])])
                cast_last = pr.op("vector", lambda e: e.tensor_copy(
                    out=wvbf[:, :], in_=wbuf[:, 2 * D:3 * D]),
                    waits=[(sT, mat_mm_done[2])], inc=sV)
            else:
                cast_last = cast_done
            next_cast_done = cast_last

            def transp_all(srcT, dst, sdone):
                last = 0
                for c in range(NCH):
                    ppar = c % 2
                    pt = ps_x0 if ppar == 0 else ps_x1
                    td = pr.op("tensor", lambda e, c=c, pt=pt, srcT=srcT: e.transpose(
                        pt[:, 0:P], srcT[:, c * P:(c + 1) * P], eye_bf[:, :]),
                        waits=[(sS, sdone), (sV, tr_ev[ppar])], inc=sT)
                    tr_ev[ppar] = pr.op("vector",
                                        lambda e, c=c, pt=pt, dst=dst: e.tensor_copy(
                                            out=dst[:, c * P:(c + 1) * P],
                                            in_=pt[:, 0:P]),
                                        waits=[(sT, td)], inc=sV)
                    last = tr_ev[ppar]
                return last

            ktd = transp_all(kT, ktok, kdn)
            vtd = transp_all(vT, vtok, vd)

            supd = pr.op("vector", lambda e: e.memset(Sbf[:, :], 0.0),
                         waits=[(sT, last_m2m3)], inc=sV)
            m2m3_done = 0
            m4_done = 0
            oev = [0, 0]
            s1ev = 0
            s3v = [0, 0]
            kdv = [0, 0]
            otr = [0, 0]
            for c in range(NCH):
                cp = c % 2
                col = c * H + h
                s3buf = s3b0 if cp == 0 else s3b1
                obuf = ob0 if cp == 0 else ob1
                kdbuf = kd0 if cp == 0 else kd1
                po = ps_o0 if cp == 0 else ps_o1
                pt = ps_x0 if cp == 0 else ps_x1

                if c < NCH - 1:
                    kdv[cp] = pr.op("scalar", lambda e, c=c, h=h, kdbuf=kdbuf:
                                    e.activation(out=kdbuf[:, :],
                                                 in_=ktok[:, c * P:(c + 1) * P],
                                                 func=AF.Identity,
                                                 scale=e3t[:, c * H + h:c * H + h + 1]),
                                    waits=[(sV, ktd), (sT, m4_done)], inc=sS)

                m1 = pr.op("tensor", lambda e, c=c: e.matmul(
                    ps_s[:, 0:P], kT[:, c * P:(c + 1) * P], qT[:, c * P:(c + 1) * P],
                    start=True, stop=True),
                    waits=[(sV, s1ev), (sV, vtd)], inc=sT)
                s1ev = pr.op("vector", lambda e, col=col, s3buf=s3buf:
                             e.scalar_tensor_tensor(
                                 out=s3buf[:, :], in0=ps_s[:, 0:P],
                                 scalar=e1t[:, col:col + 1], in1=maskb[:, :],
                                 op0=ALU.mult, op1=ALU.mult),
                             waits=[(sT, m1), (sT, m2m3_done)], inc=sV)
                s3v[cp] = s1ev

                m2m3_done = pr.op("tensor", lambda e, c=c, po=po, s3buf=s3buf: e.matmul(
                    po[:, 0:P], s3buf[:, :], vtok[:, c * P:(c + 1) * P],
                    start=True, stop=(c == 0)),
                    waits=[(sV, s3v[cp]), (sS, oev[cp])],
                    inc=(sT if c == 0 else None))
                if c > 0:
                    m2m3_done = pr.op("tensor", lambda e, c=c, po=po: e.matmul(
                        po[:, 0:P], qT[:, c * P:(c + 1) * P], Sbf[:, :],
                        start=False, stop=True),
                        waits=[(sV, supd)], inc=sT)
                last_m2m3 = m2m3_done

                if c < NCH - 1:
                    m4_done = pr.op("tensor", lambda e, c=c, kdbuf=kdbuf: e.matmul(
                        ps_S[:, 0:P], kdbuf[:, :], vtok[:, c * P:(c + 1) * P],
                        start=True, stop=True),
                        waits=[(sS, kdv[cp]), (sV, supd)], inc=sT)
                    supd = pr.op("vector", lambda e, c=c, h=h: e.scalar_tensor_tensor(
                        out=Sbf[:, :], in0=Sbf[:, :],
                        scalar=e4bc[:, c * H + h:c * H + h + 1],
                        in1=ps_S[:, 0:P], op0=ALU.mult, op1=ALU.add),
                        waits=[(sT, m4_done), (sT, m2m3_done), (sV, supd)], inc=sV)

                oev[cp] = pr.op("scalar", lambda e, col=col, po=po, obuf=obuf:
                                e.activation(out=obuf[:, :], in_=po[:, 0:P],
                                             func=AF.Identity, scale=e2t[:, col:col + 1]),
                                waits=[(sT, m2m3_done), (sV, otr[cp])], inc=sS)
                sqv = pr.op("vector", lambda e, obuf=obuf: e.scalar_tensor_tensor(
                    out=junk5[:, :], in0=obuf[:, :], scalar=1.0, in1=obuf[:, :],
                    op0=ALU.mult, op1=ALU.mult, accum_out=ssqt[:, :]),
                    waits=[(sS, oev[cp])], inc=sV)
                pr.op("vector", lambda e, c=c: e.tensor_tensor(
                    out=ssq[:, c:c + 1], in0=ssq[:, c:c + 1], in1=ssqt[:, :],
                    op=ALU.add), waits=[(sV, sqv)], inc=sV)
                otd = pr.op("tensor", lambda e, obuf=obuf, pt=pt: e.transpose(
                    pt[:, 0:P], obuf[:, :], eye_bf[:, :]),
                    waits=[(sS, oev[cp]), (sV, tr_ev[cp])], inc=sT)
                tr_ev[cp] = pr.op("vector", lambda e, c=c, h=h, pt=pt: e.tensor_copy(
                    out=oT[:, h * D + c * P: h * D + (c + 1) * P], in_=pt[:, 0:P]),
                    waits=[(sT, otd)], inc=sV)
                otr[cp] = tr_ev[cp]
            cast_done = next_cast_done
        BARRIER()
        if debug:
            for nm, tens in (("qT", qT), ("kT", kT), ("vtok", vtok), ("ktok", ktok)):
                dbg[nm] = nc.dram_tensor("dbg_" + nm, [P, D], BF16,
                                         kind="ExternalOutput").ap()
                dma("gpsimd", dbg[nm], tens[:, :], dOut)
            dbg["oT"] = nc.dram_tensor("dbg_oT", [P, H * D], BF16,
                                       kind="ExternalOutput").ap()
            dbg["ssq"] = nc.dram_tensor("dbg_ssq", [P, TT], F32,
                                        kind="ExternalOutput").ap()
            dma("gpsimd", dbg["oT"], oT[:, :], dOut)
            dma("gpsimd", dbg["ssq"], ssq[:, :], dOut)
            BARRIER()
        pr.flush()
    _sc4.close()

    wo_bf = h1T

    lnr = pr.op("scalar", lambda e: e.activation(
        out=tmp1[:, 0:TT], in_=ssq[:, :], func=AF.Ln, bias=RMSEPS, scale=1.0 / D),
        inc=sS)
    rms_done = pr.op("scalar", lambda e: e.activation(
        out=rmsr[:, :], in_=tmp1[:, 0:TT], func=AF.Exp, scale=-0.5),
        waits=[(sS, lnr)], inc=sS)

    _sc5 = ExitStack()
    wob0 = _sc5.enter_context(nc.sbuf_tensor("wobuf0", [P, D], F32, side="right"))
    wob1 = _sc5.enter_context(nc.sbuf_tensor("wobuf1", [P, D], F32, side="right"))
    xrc0 = _sc5.enter_context(nc.sbuf_tensor("xrc0", [P, D], F32, side="right"))
    xrc1 = _sc5.enter_context(nc.sbuf_tensor("xrc1", [P, D], F32, side="right"))
    x1b0 = _sc5.enter_context(nc.sbuf_tensor("x1b0", [P, D], F32, side="right"))
    x1b1 = _sc5.enter_context(nc.sbuf_tensor("x1b1", [P, D], F32, side="right"))
    gaB = _sc5.enter_context(nc.sbuf_tensor("gaB", [P, D], BF16, side="right"))
    junk6 = _sc5.enter_context(nc.sbuf_tensor("junk6", [P, D], BF16, side="right"))
    ps_a0 = _sc5.enter_context(nc.psum_tensor("ps_a0", [P, 512], F32))
    ps_a1 = _sc5.enter_context(nc.psum_tensor("ps_a1", [P, 512], F32))
    ps_a2 = _sc5.enter_context(nc.psum_tensor("ps_a2", [P, 512], F32))
    ps_a3 = _sc5.enter_context(nc.psum_tensor("ps_a3", [P, 512], F32))
    ps_gb = _sc5.enter_context(nc.psum_tensor("ps_gb", [P, 512], F32))
    if True:

        ps_a = (ps_a0, ps_a1, ps_a2, ps_a3)
        gaB_done = 0
        gev = 0
        for i in range(4):
            td = pr.op("tensor", lambda e, i=i: e.matmul(
                ps_gb[:, :], ones_b[0:1, :], garow[0:1, i * 512:(i + 1) * 512],
                start=True, stop=True), waits=[(sV, gev)], inc=sT)
            gev = pr.op("vector", lambda e, i=i: e.tensor_copy(
                out=gaB[:, i * 512:(i + 1) * 512], in_=ps_gb[:, :]),
                waits=[(sT, td)], inc=sV)
        gaB_done = gev

        wsem = (dA, dB)
        wob = (wob0, wob1)
        wo_cast = [0] * DT
        ccount = [0, 0]
        for ft in range(DT):
            par = ft % 2
            w = [(sS, ccount[par])] if ft >= 2 else []
            cnt = dma("sync", wob[par][:, :], wo_d[ft], wsem[par], waits=w)
            ccount[par] = pr.op("scalar", lambda e, ft=ft, par=par: e.activation(
                out=wo_bf[:, ft * D:(ft + 1) * D], in_=wob[par][:, :], func=AF.Identity,
                scale=nwcol[:, ft:ft + 1]),
                waits=[(wsem[par], cnt)], inc=sS)
            wo_cast[ft] = ccount[par]

        xsem6 = (dC, dE)
        x1sem = (dF, dG)
        xcnt = [0, 0]
        x1w = [0, 0]
        vread = [0, 0]
        evs = [0, 0, 0, 0]
        sst = 0
        for t in range(TT):
            par = t % 2
            xrc = xrc0 if par == 0 else xrc1
            x1buf = x1b0 if par == 0 else x1b1
            w = [(sV, vread[par])] if t >= 2 else []
            xcnt[par] = dma("sync", xrc[:, :], x_d[t], xsem6[par], waits=w)
            mmd = None
            for ft in range(DT):
                for mc in range(4):
                    mmd = pr.op("tensor", lambda e, t=t, mc=mc, ft=ft: e.matmul(
                        ps_a[mc][:, :], oT[:, ft * D + t * P:ft * D + (t + 1) * P],
                        wo_bf[:, ft * D + mc * 512: ft * D + (mc + 1) * 512],
                        start=(ft == 0), stop=(ft == DT - 1)),
                        waits=([(sS, wo_cast[ft])] if t == 0 and mc == 0 else []) +
                              ([(sV, evs[mc])] if ft == 0 else []),
                        inc=(sT if (ft == DT - 1 and mc == 3) else None))
            for mc in range(4):
                sttd = pr.op("vector", lambda e, t=t, mc=mc, x1buf=x1buf:
                             e.scalar_tensor_tensor(
                                 out=x1buf[:, mc * 512:(mc + 1) * 512],
                                 in0=ps_a[mc][:, :],
                                 scalar=rmsr[:, t:t + 1],
                                 in1=gaB[:, mc * 512:(mc + 1) * 512],
                                 op0=ALU.mult, op1=ALU.mult),
                             waits=[(sT, mmd), (sS, rms_done), (sV, gaB_done),
                                    (x1sem[par], x1w[par])], inc=sV)
                evs[mc] = pr.op("vector", lambda e, mc=mc, x1buf=x1buf, xrc=xrc:
                                e.tensor_tensor(
                                    out=x1buf[:, mc * 512:(mc + 1) * 512],
                                    in0=x1buf[:, mc * 512:(mc + 1) * 512],
                                    in1=xrc[:, mc * 512:(mc + 1) * 512],
                                    op=ALU.add),
                                waits=[(xsem6[par], xcnt[par]), (sV, sttd)], inc=sV)
            vread[par] = evs[3]
            vst = pr.op("vector", lambda e, t=t, x1buf=x1buf: e.tensor_reduce(
                out=msum[:, t:t + 1], in_=x1buf[:, :], axis=AX.X, op=ALU.add),
                waits=[(sV, evs[3])], inc=sV)
            sst = pr.op("scalar", lambda e, t=t, x1buf=x1buf: e.activation(
                out=junk6[:, :], in_=x1buf[:, :], func=AF.Square,
                accum_out=sqsum[:, t:t + 1]),
                waits=[(sV, evs[3]), (sS, sst if t else 0)], inc=sS)
            x1w[par] = dma("gpsimd", x1_hbm[t], x1buf[:, :], x1sem[par],
                           waits=[(sV, vst), (sS, sst)])

        mean2_done = pr.op("vector", lambda e: e.tensor_scalar(
            out=mean2[:, :], in0=msum[:, :], scalar1=1.0 / D, scalar2=None,
            op0=ALU.mult), waits=[(sV, sV.n), (sS, sS.n)], inc=sV)
        c1b = pr.op("vector", lambda e: e.tensor_scalar(
            out=tmp1[:, 0:TT], in0=sqsum[:, :], scalar1=1.0 / D, scalar2=None,
            op0=ALU.mult), waits=[(sV, mean2_done)], inc=sV)
        c2b = pr.op("vector", lambda e: e.tensor_tensor(
            out=tmp1[:, 16:32], in0=mean2[:, :], in1=mean2[:, :], op=ALU.mult),
            waits=[(sV, c1b)], inc=sV)
        vst2 = pr.op("vector", lambda e: e.tensor_tensor(
            out=tmp1[:, 0:TT], in0=tmp1[:, 0:TT], in1=tmp1[:, 16:32],
            op=ALU.subtract), waits=[(sV, c2b)], inc=sV)
        ln2d = pr.op("scalar", lambda e: e.activation(
            out=tmp1[:, 0:TT], in_=tmp1[:, 0:TT], func=AF.Ln, bias=LNEPS, scale=1.0),
            waits=[(sV, vst2)], inc=sS)
        pr.op("scalar", lambda e: e.activation(
            out=rstd2[:, :], in_=tmp1[:, 0:TT], func=AF.Exp, scale=-0.5),
            waits=[(sS, ln2d)], inc=sS)
        BARRIER()
        if debug:
            dbg["x1"] = nc.dram_tensor("dbg_x1", [TT, P, D], F32,
                                       kind="ExternalOutput").ap()
            for t in range(TT):
                dma("gpsimd", dbg["x1"][t], x1_hbm[t], dOut)
            BARRIER()
        pr.flush()
    _sc5.close()

    h2T = h1T

    _sc6 = ExitStack()
    x1r0 = _sc6.enter_context(nc.sbuf_tensor("x1r0", [P, D], F32, side="right"))
    x1r1 = _sc6.enter_context(nc.sbuf_tensor("x1r1", [P, D], F32, side="right"))
    lnxb20 = _sc6.enter_context(nc.sbuf_tensor("lnxb20", [P, D], BF16, side="right"))
    lnxb21 = _sc6.enter_context(nc.sbuf_tensor("lnxb21", [P, D], BF16, side="right"))
    ps_u0 = _sc6.enter_context(nc.psum_tensor("ps_u0", [P, 1024], BF16))
    ps_u1 = _sc6.enter_context(nc.psum_tensor("ps_u1", [P, 1024], BF16))
    if True:

        x1r = (x1r0, x1r1)
        xsem7 = (dA, dB)
        vdone = [0, 0]
        tile_tr_done = [0, 0]
        ev_done = [0, 0]
        for t in range(TT):
            par = t % 2
            lnxb2 = lnxb20 if par == 0 else lnxb21
            w = [(sV, vdone[par])] if t >= 2 else []
            cnt = dma("sync", x1r[par][:, :], x1_hbm[t], xsem7[par], waits=w)
            vdone[par] = pr.op("vector", lambda e, par=par, t=t, lnxb2=lnxb2: e.tensor_scalar(
                out=lnxb2[:, :], in0=x1r[par][:, :],
                scalar1=mean2[:, t:t + 1], scalar2=rstd2[:, t:t + 1],
                op0=ALU.subtract, op1=ALU.mult),
                waits=[(xsem7[par], cnt), (sS, sS.n), (sV, mean2_done),
                       (sT, tile_tr_done[par])], inc=sV)
            for dt_ in range(DT):
                ppar = dt_ % 2
                ptr = ps_u0 if ppar == 0 else ps_u1
                td = pr.op("tensor", lambda e, dt_=dt_, ptr=ptr, lnxb2=lnxb2: e.transpose(
                    ptr[:, 0:P], lnxb2[:, dt_ * P:(dt_ + 1) * P], eye_bf[:, :]),
                    waits=[(sV, vdone[par]), (sS, ev_done[ppar])], inc=sT)
                ev_done[ppar] = pr.op("scalar", lambda e, dt_=dt_, t=t, ptr=ptr:
                                      e.activation(
                                          out=h2T[:, dt_ * D + t * P: dt_ * D + (t + 1) * P],
                                          in_=ptr[:, 0:P], func=AF.Identity,
                                          scale=cbcol[:, dt_:dt_ + 1],
                                          bias=sbcol[:, dt_:dt_ + 1]),
                                      waits=[(sT, td)], inc=sS)
            tile_tr_done[par] = sT.n
        BARRIER()
        if debug:
            dbg["h2T"] = nc.dram_tensor("dbg_h2T", [P, DT * D], BF16,
                                        kind="ExternalOutput").ap()
            dma("gpsimd", dbg["h2T"], h2T[:, :], dOut)
            BARRIER()
        pr.flush()
    _sc6.close()

    _sc7 = ExitStack()
    wgb0 = _sc7.enter_context(nc.sbuf_tensor("wgbuf0", [P, 2 * D], F32, side="right"))
    wgb1 = _sc7.enter_context(nc.sbuf_tensor("wgbuf1", [P, 2 * D], F32, side="right"))
    wg0bf = _sc7.enter_context(nc.sbuf_tensor("wg0bf", [P, D], BF16, side="right"))
    wg1bf = _sc7.enter_context(nc.sbuf_tensor("wg1bf", [P, D], BF16, side="right"))
    gelb0 = _sc7.enter_context(nc.sbuf_tensor("gelb0", [P, 512], BF16, side="right"))
    gelb1 = _sc7.enter_context(nc.sbuf_tensor("gelb1", [P, 512], BF16, side="right"))
    gelb2 = _sc7.enter_context(nc.sbuf_tensor("gelb2", [P, 512], BF16, side="right"))
    gelb3 = _sc7.enter_context(nc.sbuf_tensor("gelb3", [P, 512], BF16, side="right"))
    actb0 = _sc7.enter_context(nc.sbuf_tensor("actb0", [P, 512], BF16, side="right"))
    actb1 = _sc7.enter_context(nc.sbuf_tensor("actb1", [P, 512], BF16, side="right"))
    actb2 = _sc7.enter_context(nc.sbuf_tensor("actb2", [P, 512], BF16, side="right"))
    actb3 = _sc7.enter_context(nc.sbuf_tensor("actb3", [P, 512], BF16, side="right"))
    ps_g0 = _sc7.enter_context(nc.psum_tensor("ps_g0", [P, 512], F32))
    ps_g1 = _sc7.enter_context(nc.psum_tensor("ps_g1", [P, 512], F32))
    ps_g2 = _sc7.enter_context(nc.psum_tensor("ps_g2", [P, 512], F32))
    ps_g3 = _sc7.enter_context(nc.psum_tensor("ps_g3", [P, 512], F32))
    ps_f0 = _sc7.enter_context(nc.psum_tensor("ps_f0", [P, 512], F32))
    ps_f1 = _sc7.enter_context(nc.psum_tensor("ps_f1", [P, 512], F32))
    ps_f2 = _sc7.enter_context(nc.psum_tensor("ps_f2", [P, 512], F32))
    ps_f3 = _sc7.enter_context(nc.psum_tensor("ps_f3", [P, 512], F32))
    if True:

        wsem8 = (dA, dB)
        actsem = (dF, dG, dH, dI)
        wgb = (wgb0, wgb1)
        wcnt8 = [0, 0]
        gmm = 0
        fmm = 0
        gel_done = [0, 0, 0, 0]
        act_done = [0, 0, 0, 0]
        actw = [0, 0, 0, 0]
        ps_gs = (ps_g0, ps_g1, ps_g2, ps_g3)
        ps_fs = (ps_f0, ps_f1, ps_f2, ps_f3)
        gelbs = (gelb0, gelb1, gelb2, gelb3)
        actbs = (actb0, actb1, actb2, actb3)
        wcnt8[0] = dma("sync", wgb[0][:, :].rearrange("p (m dt e) -> p m dt e",
                                                      m=2, dt=DT),
                       wg_d[0].rearrange("m dt p e -> p m dt e"), wsem8[0])
        cast0 = cast1 = 0
        cast0 = pr.op("vector", lambda e: e.tensor_copy(
            out=wg0bf[:, :], in_=wgb[0][:, 0:D]),
            waits=[(wsem8[0], wcnt8[0])], inc=sV)
        cast1 = pr.op("vector", lambda e: e.tensor_copy(
            out=wg1bf[:, :], in_=wgb[0][:, D:2 * D]), inc=sV)
        wcnt8[1] = dma("sync", wgb[1][:, :].rearrange("p (m dt e) -> p m dt e",
                                                      m=2, dt=DT),
                       wg_d[1].rearrange("m dt p e -> p m dt e"), wsem8[1],
                       waits=[(sV, cast1)])
        for s in range(NS):
            par = s % 2
            ccur0, ccur1 = cast0, cast1
            for dt_ in range(DT):
                for tc in range(4):
                    gmm = pr.op("tensor", lambda e, tc=tc, dt_=dt_: e.matmul(
                        ps_gs[tc][:, :], wg0bf[:, dt_ * P:(dt_ + 1) * P],
                        h2T[:, dt_ * D + tc * 512: dt_ * D + (tc + 1) * 512],
                        start=(dt_ == 0), stop=(dt_ == DT - 1)),
                        waits=(([(sV, ccur0)] + [(sS, g) for g in gel_done])
                               if (dt_ == 0 and tc == 0) else []),
                        inc=(sT if (dt_ == DT - 1 and tc == 3) else None))
            for tc in range(4):
                gel_done[tc] = pr.op("scalar", lambda e, tc=tc: e.activation(
                    out=gelbs[tc][:, :], in_=ps_gs[tc][:, :],
                    func=AF.Gelu_apprx_tanh),
                    waits=[(sT, gmm), (sV, act_done[tc])], inc=sS)
            if s + 1 < NS:
                cast0 = pr.op("vector", lambda e, par=par: e.tensor_copy(
                    out=wg0bf[:, :], in_=wgb[1 - par][:, 0:D]),
                    waits=[(wsem8[1 - par], wcnt8[1 - par]), (sT, gmm)], inc=sV)
            for dt_ in range(DT):
                for tc in range(4):
                    fmm = pr.op("tensor", lambda e, tc=tc, dt_=dt_: e.matmul(
                        ps_fs[tc][:, :], wg1bf[:, dt_ * P:(dt_ + 1) * P],
                        h2T[:, dt_ * D + tc * 512: dt_ * D + (tc + 1) * 512],
                        start=(dt_ == 0), stop=(dt_ == DT - 1)),
                        waits=(([(sV, ccur1)] + [(sV, a) for a in act_done])
                               if (dt_ == 0 and tc == 0) else []),
                        inc=(sT if (dt_ == DT - 1 and tc == 3) else None))
            if s + 1 < NS:
                cast1 = pr.op("vector", lambda e, par=par: e.tensor_copy(
                    out=wg1bf[:, :], in_=wgb[1 - par][:, D:2 * D]),
                    waits=[(sT, fmm)], inc=sV)
                if s + 2 < NS:
                    wcnt8[par] = dma("sync",
                                     wgb[par][:, :].rearrange(
                                         "p (m dt e) -> p m dt e", m=2, dt=DT),
                                     wg_d[s + 2].rearrange("m dt p e -> p m dt e"),
                                     wsem8[par], waits=[(sV, cast1)])
            for tc in range(4):
                act_done[tc] = pr.op("vector", lambda e, tc=tc: e.tensor_tensor(
                    out=actbs[tc][:, :], in0=gelbs[tc][:, :], in1=ps_fs[tc][:, :],
                    op=ALU.mult),
                    waits=[(sS, gel_done[tc]), (sT, fmm),
                           (actsem[tc], actw[tc])], inc=sV)
                actw[tc] = dma("gpsimd", act_hbm[tc][s], actbs[tc][:, :],
                               actsem[tc], waits=[(sV, act_done[tc])])
        BARRIER()
        pr.flush()
    _sc7.close()

    _sc8 = ExitStack()
    wlbf = h1T
    wlb0 = _sc8.enter_context(nc.sbuf_tensor("wlbuf0", [P, 8 * QW], F32, side="right"))
    wlb1 = _sc8.enter_context(nc.sbuf_tensor("wlbuf1", [P, 8 * QW], F32, side="right"))
    actcol = oT
    x1q0 = _sc8.enter_context(nc.sbuf_tensor("x1q0", [P, QW], F32, side="right"))
    x1q1 = _sc8.enter_context(nc.sbuf_tensor("x1q1", [P, QW], F32, side="right"))
    outq0 = _sc8.enter_context(nc.sbuf_tensor("outq0", [P, QW], F32, side="right"))
    outq1 = _sc8.enter_context(nc.sbuf_tensor("outq1", [P, QW], F32, side="right"))
    gbB = _sc8.enter_context(nc.sbuf_tensor("gbB", [P, D], BF16, side="right"))
    ps_y0 = _sc8.enter_context(nc.psum_tensor("ps_y0", [P, 512], F32))
    ps_y1 = _sc8.enter_context(nc.psum_tensor("ps_y1", [P, 512], F32))
    ps_y2 = _sc8.enter_context(nc.psum_tensor("ps_y2", [P, 512], F32))
    ps_y3 = _sc8.enter_context(nc.psum_tensor("ps_y3", [P, 512], F32))
    ps_y4 = _sc8.enter_context(nc.psum_tensor("ps_y4", [P, 512], F32))
    ps_y5 = _sc8.enter_context(nc.psum_tensor("ps_y5", [P, 512], F32))
    ps_y6 = _sc8.enter_context(nc.psum_tensor("ps_y6", [P, 512], F32))
    ps_y7 = _sc8.enter_context(nc.psum_tensor("ps_y7", [P, 512], F32))
    if True:

        ps_y_sets = ((ps_y0, ps_y1, ps_y2, ps_y3), (ps_y4, ps_y5, ps_y6, ps_y7))
        gev2 = 0
        for i in range(4):
            td = pr.op("tensor", lambda e, i=i: e.matmul(
                ps_y_sets[1][i][:, :], ones_b[0:1, :],
                gbrow[0:1, i * 512:(i + 1) * 512],
                start=True, stop=True), inc=sT)
            gev2 = pr.op("vector", lambda e, i=i: e.tensor_copy(
                out=gbB[:, i * 512:(i + 1) * 512], in_=ps_y_sets[1][i][:, :]),
                waits=[(sT, td)], inc=sV)
        gbB_done = gev2

        wlb = (wlb0, wlb1)
        wlsem = (dA, dB)
        x1qsem = (dF, dG)
        outsem = (dOut, dOut2)
        evq = [0, 0]
        vq_done = [[0, 0, 0, 0], [gbB_done] * 4]
        x1cnt = [0, 0]
        outw = [0, 0]
        gts = 0
        NCK = 4
        SPC = NS // NCK
        chunk_last_mm = [0] * NCK
        wl_group_last_mm = [0] * 8
        wl_cast = [0] * 8
        chunk_sem_hist = [0, 0]
        for q in range(NQ):
            gcnt = [0, 0]
            ccast = [0, 0]
            for g in range(8):
                par = g % 2
                w = [(sV, ccast[par])] if g >= 2 else []
                gcnt[par] = dma("sync",
                                wlb[par][:, :].rearrange("p (s c) -> p s c", s=8),
                                wl_d[q][g * 8:(g + 1) * 8].rearrange("s p c -> p s c"),
                                wlsem[par], waits=w)
                ccast[par] = pr.op("vector", lambda e, g=g, par=par: e.tensor_copy(
                    out=wlbf[:, g * 8 * QW:(g + 1) * 8 * QW], in_=wlb[par][:, :]),
                    waits=[(wlsem[par], gcnt[par]), (sT, wl_group_last_mm[g])],
                    inc=sV)
                wl_cast[g] = ccast[par]
            for ts in range(TSUP):
                sp = gts % 2
                ps_y = ps_y_sets[sp]
                for tt_ in range(4):
                    t = ts * 4 + tt_
                    tp = tt_ % 2
                    x1cnt[tp] = dma("gpsimd",
                                    (x1q0 if tp == 0 else x1q1)[:, :],
                                    x1_hbm[t][:, q * QW:(q + 1) * QW],
                                    x1qsem[tp], waits=[(sV, evq[tp])])
                    if tt_ == 1:
                        break
                for ck in range(NCK):
                    cpar = (gts * NCK + ck) % 2
                    csem = dC if cpar == 0 else dE
                    w = [(sT, chunk_last_mm[ck])]
                    if q == 0 and ts == 0 and ck == 0:
                        w += [(dF, dF.n), (dG, dG.n), (dH, dH.n), (dI, dI.n)]
                    if chunk_sem_hist[cpar]:
                        w += [(csem, chunk_sem_hist[cpar])]
                    cnt = dma(
                        "sync",
                        actcol[:, ck * SPC * QW:(ck + 1) * SPC * QW].rearrange(
                            "p (s c) -> p s c", s=SPC),
                        act_hbm[ts][ck * SPC:(ck + 1) * SPC].rearrange(
                            "s p c -> p s c"),
                        csem, waits=w)
                    chunk_sem_hist[cpar] = cnt
                    if ck == 0:
                        chunk_acnts = [None] * NCK
                    chunk_acnts[ck] = (csem, cnt)
                mmd = 0
                for s in range(NS):
                    ck = s // SPC
                    for tt_ in range(4):
                        w = []
                        if s % SPC == 0 and tt_ == 0:
                            w += [chunk_acnts[ck]]
                        if s % 8 == 0 and tt_ == 0 and ts == 0:
                            w += [(sV, wl_cast[s // 8])]
                        if s == 0:
                            w += [(sV, vq_done[sp][tt_])]
                        do_inc = (tt_ == 3 and (s % 8 == 7))
                        mmd = pr.op("tensor", lambda e, s=s, tt_=tt_, py=ps_y[tt_]: e.matmul(
                            py[:, :],
                            actcol[:, s * QW + tt_ * P: s * QW + (tt_ + 1) * P],
                            wlbf[:, s * QW:(s + 1) * QW],
                            start=(s == 0), stop=(s == NS - 1)),
                            waits=w,
                            inc=(sT if do_inc else None))
                    if s % 8 == 7:
                        wl_group_last_mm[s // 8] = mmd
                    if s % SPC == SPC - 1:
                        chunk_last_mm[s // SPC] = mmd
                for tt_ in range(4):
                    t = ts * 4 + tt_
                    tp = tt_ % 2
                    x1q = x1q0 if tp == 0 else x1q1
                    outq = outq0 if tp == 0 else outq1
                    vq = pr.op("vector", lambda e, q=q, outq=outq, py=ps_y[tt_]:
                               e.tensor_tensor(
                                   out=outq[:, :], in0=py[:, :],
                                   in1=gbB[:, q * QW:(q + 1) * QW], op=ALU.mult),
                               waits=[(sT, mmd), (sV, gbB_done),
                                      (x1qsem[tp], x1cnt[tp]),
                                      (outsem[tp], outw[tp])], inc=sV)
                    vq_done[sp][tt_] = vq
                    evq[tp] = pr.op("vector", lambda e, outq=outq, x1q=x1q:
                                    e.tensor_tensor(out=outq[:, :], in0=outq[:, :],
                                                    in1=x1q[:, :], op=ALU.add),
                                    waits=[(sV, vq)], inc=sV)
                    outw[tp] = dma("gpsimd", y_d[t][:, q * QW:(q + 1) * QW],
                                   outq[:, :], outsem[tp], waits=[(sV, evq[tp])])
                    if tt_ < 2 and (ts + 1 < TSUP or q + 1 < NQ):
                        pass
                    if tt_ == 1:
                        for tt2 in (2, 3):
                            t2 = ts * 4 + tt2
                            tp2 = tt2 % 2
                            x1cnt[tp2] = dma("gpsimd",
                                             (x1q0 if tp2 == 0 else x1q1)[:, :],
                                             x1_hbm[t2][:, q * QW:(q + 1) * QW],
                                             x1qsem[tp2], waits=[(sV, evq[tp2])])
                gts += 1
        pr.op("gpsimd", lambda e: e.engine_nop(),
              waits=[(dOut, dOut.n), (dOut2, dOut2.n)])
        pr.flush()

    _sc8.close()
    es.close()
    return nc, dbg


_BUILT = None


def _host_tiles(inputs):
    f32 = np.float32
    Wq = np.asarray(inputs["L0_attn_Wq"], f32)
    Wk = np.asarray(inputs["L0_attn_Wk"], f32)
    Wv = np.asarray(inputs["L0_attn_Wv"], f32)
    wqkv = np.ascontiguousarray(np.stack([
        Wq.reshape(DT, P, H, E).transpose(2, 0, 1, 3),
        Wk.reshape(DT, P, H, E).transpose(2, 0, 1, 3),
        Wv.reshape(DT, P, H, E).transpose(2, 0, 1, 3),
    ], axis=1))
    wmod = np.ascontiguousarray(
        np.asarray(inputs["L0_wmod"], f32).reshape(DT, P, NMS, MSW).transpose(2, 0, 1, 3))
    ccol = np.ascontiguousarray(
        np.asarray(inputs["L0_c"], f32).reshape(8, DT, P).transpose(0, 2, 1))
    wf = np.ascontiguousarray(np.asarray(inputs["L0_attn_Wf"], f32).reshape(DT, P, H))
    wo = np.ascontiguousarray(np.asarray(inputs["L0_attn_Wo"], f32).reshape(DT, P, D))
    nwcol_ = np.ascontiguousarray(
        np.asarray(inputs["L0_attn_norm_w"], f32).reshape(DT, P).T)
    wg_in = np.asarray(inputs["L0_ffn_w_gating"], f32)
    wg = np.ascontiguousarray(
        np.stack([wg_in[0].reshape(DT, P, NS, E).transpose(2, 0, 1, 3),
                  wg_in[1].reshape(DT, P, NS, E).transpose(2, 0, 1, 3)],
                 axis=1))
    wl = np.ascontiguousarray(
        np.asarray(inputs["L0_ffn_w_linear"], f32).reshape(NS, P, NQ, QW).transpose(2, 0, 1, 3))
    return wqkv, wmod, ccol, wf, wo, nwcol_, wg, wl


def kernel(**inputs):
    global _BUILT
    if _BUILT is None:
        _BUILT = build(debug=False)
    nc, _ = _BUILT
    wqkv, wmod, ccol, wf, wo, nwcol_, wg, wl = _host_tiles(inputs)
    x = np.ascontiguousarray(np.asarray(inputs["L0_attn_x"], np.float32))
    in_maps = []
    for b in range(8):
        in_maps.append({
            "x": x[b].reshape(TT, P, D),
            "wmod": wmod, "ccol": np.ascontiguousarray(ccol[b]),
            "wqkv": wqkv, "wf": wf, "wo": wo, "nwcol": nwcol_,
            "wg": wg, "wl": wl,
        })
    res = run_bass_kernel_spmd(nc, in_maps, core_ids=list(range(8)))
    out = np.stack([r["y"].reshape(D, D) for r in res.results])
    return out
